# version 13
# baseline (speedup 1.0000x reference)
import numpy as np
import ml_dtypes
import concourse.bass as bass
import concourse.mybir as mybir
from concourse.bass_utils import run_bass_kernel_spmd

F32 = mybir.dt.float32
BF16 = mybir.dt.bfloat16
AF = mybir.ActivationFunctionType
ALU = mybir.AluOpType
AX = mybir.AxisListType
BF = ml_dtypes.bfloat16

NCORE = 8
S = 8192
E = 1024
NH = 16
D = 64
T = 128
NSC_FULL = 16
SCT = 512
DFF = 4096
RK = 256


class Ctr:

    def __init__(self, nc, name):
        self.h = nc.alloc_semaphore(name)
        self.n = 0

    def snap(self):
        return (self, self.n)


class Prog:

    ENGINES = ("t", "v", "s", "y", "g")

    def __init__(self):
        self.steps = []

    def do(self, eng, fn, waits=(), incs=()):
        w = [x if isinstance(x, tuple) else x.snap() for x in waits]
        self.steps.append((eng, w, fn, list(incs)))
        for (c, k) in incs:
            c.n += k

    def emit(self, block):
        def mk(eng_key):
            def run(engine):
                done = {}
                for (ek, waits, fn, incs) in self.steps:
                    if ek != eng_key:
                        continue
                    for (c, val) in waits:
                        if val > 0 and done.get(id(c), -1) < val:
                            engine.wait_ge(c.h, val)
                            done[id(c)] = val
                    instr = fn(engine)
                    for (c, k) in incs:
                        instr.then_inc(c.h, k)
            return run

        block.tensor(mk("t"))
        block.vector(mk("v"))
        block.scalar(mk("s"))
        block.sync(mk("y"))
        block.gpsimd(mk("g"))


def build_nc(nsc=NSC_FULL, debug=False):
    nc = bass.Bass("TRN2", target_bir_lowering=False)
    ap = lambda t: t.ap()

    x_h = ap(nc.dram_tensor("L0_attn_hidden_states", [S, E], F32, kind="ExternalInput"))
    wq_h = ap(nc.dram_tensor("L0_attn_wq", [E, E], F32, kind="ExternalInput"))
    bq_h = ap(nc.dram_tensor("L0_attn_bq", [E], F32, kind="ExternalInput"))
    wk_h = ap(nc.dram_tensor("L0_attn_wk", [E, E], F32, kind="ExternalInput"))
    bk_h = ap(nc.dram_tensor("L0_attn_bk", [E], F32, kind="ExternalInput"))
    wv_h = ap(nc.dram_tensor("L0_attn_wv", [E, E], F32, kind="ExternalInput"))
    bv_h = ap(nc.dram_tensor("L0_attn_bv", [E], F32, kind="ExternalInput"))
    wo_h = ap(nc.dram_tensor("L0_attn_wo", [E, E], F32, kind="ExternalInput"))
    bo_h = ap(nc.dram_tensor("L0_attn_bo", [E], F32, kind="ExternalInput"))
    u1_h = ap(nc.dram_tensor("L0_ffn_U1", [E, RK], F32, kind="ExternalInput"))
    v1_h = ap(nc.dram_tensor("L0_ffn_V1", [RK, DFF], F32, kind="ExternalInput"))
    b1_h = ap(nc.dram_tensor("L0_ffn_b1", [DFF], F32, kind="ExternalInput"))
    u2_h = ap(nc.dram_tensor("L0_ffn_U2", [DFF, RK], F32, kind="ExternalInput"))
    v2_h = ap(nc.dram_tensor("L0_ffn_V2", [RK, E], F32, kind="ExternalInput"))
    b2_h = ap(nc.dram_tensor("L0_ffn_b2", [E], F32, kind="ExternalInput"))
    out_h = ap(nc.dram_tensor("out", [S, E], F32, kind="ExternalOutput"))
    lse_scr = [ap(nc.dram_tensor(f"lse_scr{i}", [32, 256], BF16, kind="Internal"))
               for i in range(2)]

    ident_c = nc.inline_tensor(np.eye(128, dtype=np.float32), name="c_ident")
    identb_c = nc.inline_tensor(np.eye(128, dtype=BF), name="c_identb")
    o2d = np.zeros((34, 128), dtype=BF)
    o2d[0, :] = 1
    o2d[1, :] = 1
    o2d[32, :] = 1
    o2d[33, :] = 1
    ones2_c = nc.inline_tensor(o2d, name="c_ones2")
    sel4 = np.zeros((68, 128), dtype=BF)
    for b0 in (0, 64):
        sel4[b0 + 0, 0:64] = 1
        sel4[b0 + 1, 64:128] = 1
        sel4[b0 + 2, 0:64] = 1
        sel4[b0 + 3, 64:128] = 1
    sel4_c = nc.inline_tensor(sel4, name="c_sel4")
    onescol_c = nc.inline_tensor(np.ones((128, 1), dtype=BF), name="c_onescol")
    cv = np.zeros((128, 8), dtype=np.float32)
    cv[:, 0] = 1e-5
    cv[:, 1] = -0.5
    cv[:, 2] = 0.125
    cv[:, 3] = 1.0 / 128
    cv[:, 4] = 128.0
    cv[:, 5] = -20.0
    cv[:, 6] = 1.0 / 1024
    cvals_c = nc.inline_tensor(cv, name="c_cvals")

    sb = nc.alloc_sbuf_tensor
    wqT = sb("wqT", [128, 8192], BF16)
    wkT = sb("wkT", [128, 8192], BF16)
    wvT = sb("wvT", [128, 8192], BF16)
    woT = sb("woT", [128, 8192], BF16)
    U1 = sb("U1", [128, 2048], BF16)
    V1 = sb("V1", [128, 8192], BF16)
    U2 = sb("U2", [128, 8192], BF16)
    V2 = sb("V2", [128, 2048], BF16)
    x_buf = sb("x_buf", [128, 8192], F32)
    out_st = sb("out_st", [128, 2048], F32)
    xn_fm = sb("xn_fm", [128, 4096], BF16)
    q_fm = sb("q_fm", [128, 4096], BF16)
    k_fm = sb("k_fm", [128, 4096], BF16)
    v_tm = sb("v_tm", [128, 4096], BF16)
    attn_fm = sb("attn_fm", [128, 4096], BF16)
    xn_tm = v_tm
    A1T = sb("A1T", [128, 2048], BF16)
    A2T = sb("A2T", [128, 1024], BF16)
    H_rot = sb("H_rot", [128, 2048], BF16)
    P_fm = sb("P_fm", [128, 1024], BF16)
    Q_fm = sb("Q_fm", [128, 1024], BF16)
    km_fm = sb("km_fm", [128, 512], BF16)
    vm_cm = sb("vm_cm", [128, 1024], BF16)
    ident = sb("identity", [128, 128], F32)
    ident_b = sb("identity_b", [128, 128], BF16)
    ones2 = sb("ones2", [34, 128], BF16)
    sel4_t = sb("sel4", [68, 128], BF16)
    ones_col = sb("ones_col", [128, 1], BF16)
    cvals = sb("cvals", [128, 8], F32)
    bq_eff = sb("bq_eff", [128, 8], F32)
    bk_f = sb("bk_f", [128, 8], F32)
    b1_f = sb("b1_f", [128, 32], F32)
    bv_row = sb("bv_row", [1, 1024], BF16)
    bob2_row = sb("bob2_row", [1, 1024], BF16)
    stats = sb("stats", [128, 16], F32)
    negm1 = sb("negm1", [128, 16], F32)
    negm2 = sb("negm2", [128, 16], F32)
    z1 = sb("z1", [128, 16], F32)
    z2 = sb("z2", [128, 16], F32)
    z2b = negm1
    lnz = sb("lnz", [128, 32], F32)
    neglse = sb("neglse", [128, 32], F32)
    rows_cd = sb("rows_cd", [32, 256], BF16)
    kcol = sb("kcol", [128, 1], F32)
    bias_r = out_st[0:32, 0:128]
    flat_bufs = {0: (H_rot[0:2, 0:2048], H_rot[64:68, 0:1024]),
                 1: (H_rot[32:34, 0:2048], Q_fm[0:4, 0:1024])}
    vrow_slots = None

    c_t = Ctr(nc, "c_t")
    c_v = Ctr(nc, "c_v")
    c_s = Ctr(nc, "c_s")
    d_xp = [Ctr(nc, "d_x0"), Ctr(nc, "d_x1")]
    d_cst = Ctr(nc, "d_cst")
    d_ffw = Ctr(nc, "d_ffw")
    d_st = [Ctr(nc, "d_st0"), Ctr(nc, "d_st1")]
    d_lst = Ctr(nc, "d_lst")
    d_lsl = Ctr(nc, "d_lsl")
    d_vm = [Ctr(nc, "d_vm0"), Ctr(nc, "d_vm1")]
    d_op = [Ctr(nc, "d_o0"), Ctr(nc, "d_o1")]

    ps = Prog()
    if debug:
        mlist = (km_fm, vm_cm, A1T, A2T, H_rot, P_fm, Q_fm, attn_fm,
                 q_fm, k_fm, v_tm, xn_fm, negm1, negm2, z1, z2, lnz,
                 neglse, rows_cd, stats, out_st, x_buf, kcol)
        for i, tt in enumerate(mlist):
            ps.do("v", (lambda tt=tt: lambda e: e.memset(tt[:], 0.0))(),
                  incs=([(c_v, 1)] if i == len(mlist) - 1 else ()))

    def dma(eng_attr, out, in_):
        return lambda e: e.dma_start(out=out, in_=in_)

    for (dst, src) in ((ident, ident_c), (ident_b, identb_c), (ones2, ones2_c),
                       (sel4_t, sel4_c), (ones_col, onescol_c), (cvals, cvals_c)):
        ps.do("y", dma("y", dst[:], src.ap()), incs=[(d_cst, 16)])
    xr = x_h.rearrange("(sc st p) f -> sc p st f", p=128, st=4)
    ps.do("y", dma("y", x_buf[:, 0:4096].rearrange("p (st f) -> p st f", st=4),
                   xr[0]), waits=[c_v], incs=[(d_xp[0], 16)])

    ps.do("g", dma("g", U1[:].rearrange("p (dc r) -> p dc r", dc=8),
                   u1_h.rearrange("(dc p) r -> p dc r", p=128)),
          waits=[c_v], incs=[(d_ffw, 16)])
    ps.do("g", dma("g", V1[:].rearrange("p (rc f) -> p rc f", rc=2),
                   v1_h.rearrange("(rc p) f -> p rc f", p=128)), incs=[(d_ffw, 16)])
    ps.do("g", dma("g", U2[:].rearrange("p (gc r) -> p gc r", gc=32),
                   u2_h.rearrange("(gc p) r -> p gc r", p=128)), incs=[(d_ffw, 16)])
    ps.do("g", dma("g", V2[:].rearrange("p (rc d) -> p rc d", rc=2),
                   v2_h.rearrange("(rc p) d -> p rc d", p=128)), incs=[(d_ffw, 16)])
    ps.do("g", dma("g", bias_r[0:8, :], bq_h.rearrange("(a t) -> a t", t=128)),
          incs=[(d_ffw, 16)])

    ps.do("g", dma("g", bv_row[:], bv_h.rearrange("(a t) -> a t", t=1024)),
          incs=[(d_ffw, 16)])
    ps.do("g", dma("g", out_st[64:65, 0:1024], bo_h.rearrange("(a t) -> a t", t=1024)),
          incs=[(d_ffw, 16)])
    ps.do("g", dma("g", out_st[64:65, 1024:2048], b2_h.rearrange("(a t) -> a t", t=1024)),
          incs=[(d_ffw, 16)])

    ps.do("v", lambda e: e.tensor_tensor(bob2_row[:], out_st[64:65, 0:1024],
                                         out_st[64:65, 1024:2048], ALU.add),
          waits=[d_ffw], incs=[(c_v, 1)])

    pb = nc.alloc_psum_tensor
    B = [pb(f"B{i}", [128, 512], F32) for i in range(8)]

    for bi in range(8):
        ps.do("v", (lambda bi=bi: lambda e: e.memset(B[bi][:], 0.0))(),
              incs=([(c_v, 1)] if bi == 7 else ()))
    dbg_v0 = c_v.n
    wlist = [(wq_h, wqT, 2), (wk_h, wkT, None), (wv_h, wvT, None),
             (wo_h, woT, None)]
    stage = H_rot
    ev_after = {}
    for wi, (wsrc, dstT, scale_col) in enumerate(wlist):
        for fc in range(8):
            slot = (wi * 8 + fc) % 2
            stg = stage[:, slot * 1024:(slot + 1) * 1024]
            w_st = [(c_t, ev_after.get(slot, 0)), (c_v, dbg_v0)]
            ps.do("g", dma("g", stg, wsrc[fc * 128:(fc + 1) * 128, :]),
                  waits=w_st, incs=[(d_st[slot], 16)])
            wts = [(d_st[slot], d_st[slot].n), (d_cst, d_cst.n),
                   (c_v, ev_after.get(("e", slot), dbg_v0))]
            for half in range(2):
                bank = B[slot * 2 + half]
                for eci in range(4):
                    ec = half * 4 + eci
                    ps.do("t", (lambda b=bank, sg=stg, ec=ec, eci=eci: lambda e:
                                e.matmul(b[:, eci * 128:(eci + 1) * 128],
                                         sg[:, ec * 128:(ec + 1) * 128],
                                         ident_b[:], start=(eci == 0),
                                         stop=(eci == 3)))(),
                          waits=(wts if (half == 0 and eci == 0) else ()),
                          incs=([(c_t, 1)] if (half == 1 and eci == 3) else ()))
            ev_after[slot] = c_t.n
            def mkev(slot=slot, dstT=dstT, fc=fc, scale_col=scale_col):
                def f(e):
                    o = dstT[:, 0:8192].rearrange("p (ec f) -> p ec f", ec=8)[
                        :, :, fc * 128:(fc + 1) * 128]
                    i0 = B[slot * 2][:].rearrange("p (a b) -> p a b", a=4)
                    i1 = B[slot * 2 + 1][:].rearrange("p (a b) -> p a b", a=4)
                    if scale_col is not None:
                        e.tensor_scalar(o[:, 0:4], i0,
                                        cvals[:, scale_col:scale_col + 1],
                                        None, ALU.mult)
                        return e.tensor_scalar(o[:, 4:8], i1,
                                               cvals[:, scale_col:scale_col + 1],
                                               None, ALU.mult)
                    e.tensor_copy(o[:, 0:4], i0)
                    return e.tensor_copy(o[:, 4:8], i1)
                return f
            ps.do("v", mkev(), waits=[c_t], incs=[(c_v, 1)])
            ev_after[("e", slot)] = c_v.n

    with nc.Block("setup") as blk1:
        ps.emit(blk1)

    setup_t, setup_v, setup_s = c_t.n, c_v.n, c_s.n

    pm = Prog()

    pm.do("t", lambda e: e.transpose(B[0][0:128, 0:8], bias_r[0:8, :], ident[0:8, 0:8]),
          waits=[(c_v, setup_v)], incs=[(c_t, 1)])
    t_bias0 = c_t.n
    pm.do("g", dma("g", bias_r[0:8, :], bk_h.rearrange("(a t) -> a t", t=128)),
          waits=[(c_t, t_bias0)], incs=[(d_ffw, 16)])
    pm.do("t", lambda e: e.transpose(B[0][0:128, 8:16], bias_r[0:8, :], ident[0:8, 0:8]),
          waits=[(d_ffw, d_ffw.n)], incs=[(c_t, 1)])
    t_bias1 = c_t.n
    pm.do("v", lambda e: e.tensor_scalar(bq_eff[:], B[0][:, 0:8],
                                         cvals[:, 2:3], None, ALU.mult),
          waits=[(c_t, t_bias1)], incs=[(c_v, 1)])
    pm.do("v", lambda e: e.tensor_copy(bk_f[:], B[0][:, 8:16]), incs=[(c_v, 1)])
    v_bias1 = c_v.n
    pm.do("g", dma("g", bias_r[:], b1_h.rearrange("(a t) -> a t", t=128)),
          waits=[(c_v, v_bias1)], incs=[(d_ffw, 16)])
    pm.do("t", lambda e: e.transpose(B[0][0:128, 16:48], bias_r[:], ident[0:32, 0:32]),
          waits=[(d_ffw, d_ffw.n), (c_v, v_bias1)], incs=[(c_t, 1)])
    pm.do("v", lambda e: e.tensor_copy(b1_f[:], B[0][:, 16:48]),
          waits=[c_t], incs=[(c_v, 1)])

    bank_gen = [c_t.snap()[1]] * 8
    bank_free = [(c_v, 0)] * 8

    def after_all(*cs):
        return [c.snap() for c in cs]

    x_guard = {0: (c_v, 0), 1: (c_v, 0)}
    lse_guard = {0: (c_t, 0), 1: (c_t, 0)}
    flat_guard = {0: (c_t, 0), 1: (c_t, 0)}
    vrow_guard = {0: (d_vm[0], 0), 1: (d_vm[1], 0)}
    vm_ready = [(d_vm[0], 0), (d_vm[1], 0)]
    a_guard = (c_t, 0)
    stage_guard = {0: (d_op[0], 0), 1: (d_op[1], 0)}
    kq_guard = (c_t, 0)
    xnfm_guard = (c_t, 0)
    ones2_sl = {0: ones2[0:2, :], 1: ones2[32:34, :]}
    sel4_sl = {0: sel4_t[64:68, :], 1: sel4_t[0:4, :]}
    vrow_slots = {0: A1T[0:1, 0:1024], 1: A2T[0:1, 0:1024]}
    av_done_prev = [0]
    exp_done_st = [{}]
    neglse_guard = [(c_t, 0)]
    attn_guard = (c_v, 0)
    hrot_guard = [(c_t, 0)] * 4
    hflat_guard = (c_t, 0)
    pq_guard = (c_t, 0)

    def emit_ln_sums(scx):
        parx = scx % 2
        xbx = x_buf[:, parx * 4096:(parx + 1) * 4096]
        for st in range(4):
            xt = xbx[:, st * 1024:(st + 1) * 1024]
            s_sum = stats[:, st * 4:st * 4 + 1]
            s_nm = stats[:, st * 4 + 2:st * 4 + 3]
            pm.do("v", (lambda xt=xt, s_sum=s_sum: lambda e:
                        e.reduce_sum(s_sum, xt, axis=AX.X))(),
                  waits=[d_xp[parx], x_guard[parx], c_v.snap()], incs=[(c_v, 1)])
            pm.do("v", (lambda s_sum=s_sum, s_nm=s_nm: lambda e:
                        e.tensor_scalar(s_nm, s_sum, -1.0 / 1024, None,
                                        ALU.mult))(),
                  waits=[c_v], incs=[(c_v, 1)])
        return c_v.n

    def emit_ln(scx, sums_v):
        parx = scx % 2
        xbx = x_buf[:, parx * 4096:(parx + 1) * 4096]
        ln_done = []
        for st in range(4):
            xt = xbx[:, st * 1024:(st + 1) * 1024]
            xnt = xn_tm[:, st * 1024:(st + 1) * 1024]
            s_sq = stats[:, st * 4 + 1:st * 4 + 2]
            s_nm = stats[:, st * 4 + 2:st * 4 + 3]
            s_rs = stats[:, st * 4 + 3:st * 4 + 4]
            pm.do("s", (lambda xt=xt, xnt=xnt, s_sq=s_sq, s_nm=s_nm: lambda e:
                        e.activation(xnt, xt, AF.Square, bias=s_nm,
                                     accum_out=s_sq))(),
                  waits=[(c_v, sums_v), kq_guard, c_s.snap()], incs=[(c_s, 1)])
            pm.do("s", (lambda s_sq=s_sq, s_rs=s_rs: lambda e:
                        e.activation(s_rs, s_sq, AF.Ln, bias=cvals[:, 0:1],
                                     scale=cvals[:, 6:7]))(),
                  waits=[c_s], incs=[(c_s, 1)])
            pm.do("s", (lambda s_rs=s_rs: lambda e:
                        e.activation(s_rs, s_rs, AF.Exp, scale=cvals[:, 1:2]))(),
                  waits=[c_s], incs=[(c_s, 1)])
            sw2 = [c_s.snap()]
            pm.do("v", (lambda xt=xt, xnt=xnt, s_nm=s_nm, s_rs=s_rs: lambda e:
                        e.tensor_scalar(xnt, xt, s_nm, s_rs, ALU.add,
                                        ALU.mult))(),
                  waits=sw2 + [c_v.snap()], incs=[(c_v, 1)])
            ln_done.append(c_v.n)
        return ln_done


    ln_next = emit_ln(0, emit_ln_sums(0))

    for sc in range(nsc):
        par = sc % 2
        xb = x_buf[:, par * 4096:(par + 1) * 4096]

        if sc + 1 < nsc:
            npar = (sc + 1) % 2
            pm.do("y", (lambda sc=sc, npar=npar: lambda e: e.dma_start(
                        out=x_buf[:, npar * 4096:(npar + 1) * 4096].rearrange(
                            "p (st f) -> p st f", st=4),
                        in_=xr[sc + 1]))(),
                  waits=[x_guard[npar]], incs=[(d_xp[npar], 16)])

        ln_done = ln_next

        for st in range(4):
            bks = (4, 5) if st % 2 else (6, 7)
            for half in range(2):
                bk = bks[half]
                w = [(c_v, ln_done[st]), bank_free[bk]]
                for eci in range(4):
                    ec = half * 4 + eci
                    pm.do("t", (lambda bk=bk, st=st, ec=ec, eci=eci: lambda e:
                                e.matmul(B[bk][:, eci * 128:(eci + 1) * 128],
                                         xn_tm[:, st * 1024 + ec * 128:
                                               st * 1024 + ec * 128 + 128],
                                         ident_b[:], start=(eci == 0),
                                         stop=(eci == 3)))(),
                          waits=(w if eci == 0 else ()),
                          incs=([(c_t, 1)] if eci == 3 else ()))
                bank_gen[bk] = c_t.n
                pm.do("v", (lambda bk=bk, st=st, half=half: lambda e:
                            e.tensor_copy(
                                xn_fm[:, 0:4096].rearrange(
                                    "p (ec f) -> p ec f", ec=8)[
                                    :, half * 4:half * 4 + 4,
                                    st * 128:st * 128 + 128],
                                B[bk][:].rearrange("p (a b) -> p a b", a=4)))(),
                      waits=[(c_t, bank_gen[bk]), xnfm_guard], incs=[(c_v, 1)])
                bank_free[bk] = (c_v, c_v.n)
        xn_ready = c_v.n

        for (wT, dst, brow) in ((wqT, q_fm, bq_eff), (wkT, k_fm, bk_f)):
            for fc in range(8):
                bk = fc % 4
                w = [(c_v, xn_ready), bank_free[bk], (c_s, setup_s)]
                for ec in range(8):
                    pm.do("t", (lambda bk=bk, wT=wT, fc=fc, ec=ec: lambda e:
                                e.matmul(B[bk][:],
                                         wT[:, ec * 1024 + fc * 128:
                                            ec * 1024 + fc * 128 + 128],
                                         xn_fm[:, ec * 512:(ec + 1) * 512],
                                         start=(ec == 0), stop=(ec == 7)))(),
                          waits=(w if ec == 0 else ()),
                          incs=([(c_t, 1)] if ec == 7 else ()))
                bank_gen[bk] = c_t.n
                pm.do("s", (lambda bk=bk, dst=dst, fc=fc, brow=brow: lambda e:
                            e.activation(dst[:, fc * 512:(fc + 1) * 512], B[bk][:],
                                         AF.Identity, bias=brow[:, fc:fc + 1]))(),
                      waits=[(c_t, bank_gen[bk]), kq_guard], incs=[(c_s, 1)])
                bank_free[bk] = (c_s, c_s.n)
        for st in range(4):
            for half in range(2):
                bk = (st * 2 + half) % 4
                w = [(c_v, xn_ready), bank_free[bk]]
                for ec in range(8):
                    pm.do("t", (lambda bk=bk, st=st, half=half, ec=ec: lambda e:
                                e.matmul(B[bk][:],
                                         xn_fm[:, ec * 512 + st * 128:
                                               ec * 512 + st * 128 + 128],
                                         wvT[:, ec * 1024 + half * 512:
                                             ec * 1024 + half * 512 + 512],
                                         start=(ec == 0), stop=False))(),
                          waits=(w if ec == 0 else ()))
                pm.do("t", (lambda bk=bk, half=half: lambda e:
                            e.matmul(B[bk][:], ones2[0:1, :],
                                     bv_row[0:1, half * 512:half * 512 + 512],
                                     start=False, stop=True))(),
                      incs=[(c_t, 1)])
                bank_gen[bk] = c_t.n
                pm.do("v", (lambda bk=bk, st=st, half=half: lambda e:
                            e.tensor_copy(v_tm[:, st * 1024 + half * 512:
                                               st * 1024 + half * 512 + 512],
                                          B[bk][:]))(),
                      waits=[(c_t, bank_gen[bk]), kq_guard], incs=[(c_v, 1)])
                bank_free[bk] = (c_v, c_v.n)
        v_done = c_v.n
        q_done_s = c_s.n

        if sc + 1 < nsc:
            ln_sums_next = emit_ln_sums(sc + 1)

        km_done = []
        for cc in range(4):
            c = 4 * sc + cc
            for fc in range(8):
                pm.do("v", (lambda fc=fc, cc=cc: lambda e:
                            e.reduce_sum(kcol[:], k_fm[:, fc * 512 + cc * 128:
                                                       fc * 512 + cc * 128 + 128],
                                         axis=AX.X))(),
                      waits=[(c_s, q_done_s), (c_v, c_v.n)], incs=[(c_v, 1)])
                pm.do("v", (lambda fc=fc, c=c: lambda e:
                            e.tensor_scalar(km_fm[:, fc * 64 + c:fc * 64 + c + 1],
                                            kcol[:], 1.0 / 128, None, ALU.mult))(),
                      waits=[c_v], incs=[(c_v, 1)])
            km_done.append(c_v.n)

        vm_latest = [[(d_vm[0], 0), (d_vm[1], 0)]]

        def emit_vsum(cc):
            c = 4 * sc + cc
            vr = vrow_slots[cc % 2]
            w = [(c_v, v_done), bank_free[6], bank_free[7], vrow_guard[cc % 2],
                 hflat_guard, a_guard]
            pm.do("t", (lambda cc=cc: lambda e:
                        e.matmul(B[6][0:1, :], ones_col[:],
                                 v_tm[:, cc * 1024:cc * 1024 + 512],
                                 start=True, stop=True))(),
                  waits=w, incs=[(c_t, 1)])
            pm.do("t", (lambda cc=cc: lambda e:
                        e.matmul(B[7][0:1, :], ones_col[:],
                                 v_tm[:, cc * 1024 + 512:cc * 1024 + 1024],
                                 start=True, stop=True))(),
                  incs=[(c_t, 1)])
            bank_gen[6] = bank_gen[7] = c_t.n
            pm.do("s", (lambda vr=vr: lambda e:
                        e.activation(vr[0:1, 0:512], B[6][0:1, :], AF.Copy))(),
                  waits=[(c_t, c_t.n)], incs=[(c_s, 1)])
            pm.do("s", (lambda vr=vr: lambda e:
                        e.activation(vr[0:1, 512:1024], B[7][0:1, :], AF.Copy))(),
                  incs=[(c_s, 1)])
            bank_free[6] = bank_free[7] = (c_s, c_s.n)
            sw = [c_s.snap()]
            pm.do("g", (lambda c=c, vr=vr: lambda e:
                        e.dma_start(out=vm_cm[c:c + 1, :], in_=vr))(),
                  waits=sw, incs=[(d_vm[cc % 2], 16)])
            pm.do("g", (lambda c=c, vr=vr: lambda e:
                        e.dma_start(out=vm_cm[64 + c:65 + c, :], in_=vr))(),
                  incs=[(d_vm[cc % 2], 16)])
            vrow_guard[cc % 2] = (d_vm[cc % 2], d_vm[cc % 2].n)
            vm_latest[0] = [(d_vm[0], d_vm[0].n), (d_vm[1], d_vm[1].n)]
            return vm_latest[0]

        cix = lambda h: (h % 2) * 8 + h // 2
        grp_heads = [[h for h in range(16) if cix(h) // 4 == g] for g in range(4)]
        pool_i = [0]

        def pool_next():
            bk = pool_i[0]
            pool_i[0] = (bk + 1) % 6
            import os
            if os.environ.get("POOL_TRACE"):
                print("pool ->", bk)
            return bk

        def emit_p1(cc):
            c = 4 * sc + cc
            ncj = c + 1
            first = [(c_s, q_done_s), (c_v, km_done[cc]), a_guard]
            for kind, g in (("i", 0), ("i", 1), ("n", 0), ("i", 2), ("i", 3),
                            ("n", 1)):
                bk = pool_next()
                w = [bank_free[bk]] + first
                first = []
                if kind == "i":
                    for qi, h in enumerate(grp_heads[g]):
                        base = (h % 2) * 64
                        fc = h // 2
                        pm.do("t", (lambda bk=bk, qi=qi, base=base, fc=fc,
                                    cc=cc: lambda e: e.matmul(
                                        B[bk][:, qi * 128:(qi + 1) * 128],
                                        q_fm[base:base + 64, fc * 512 + cc * 128:
                                             fc * 512 + cc * 128 + 128],
                                        k_fm[base:base + 64, fc * 512 + cc * 128:
                                             fc * 512 + cc * 128 + 128],
                                        start=(qi == 0), stop=(qi == 3),
                                        skip_group_check=True))(),
                              waits=(w if qi == 0 else ()),
                              incs=([(c_t, 1)] if qi == 3 else ()))
                    pm.do("s", (lambda bk=bk: lambda e:
                                e.activation(B[bk][:], B[bk][:], AF.Exp,
                                             bias=cvals[:, 5:6]))(),
                          waits=[(c_t, c_t.n)], incs=[(c_s, 1)])
                    pm.do("v", (lambda bk=bk, g=g: lambda e:
                                e.reduce_sum(z1[:, 4 * g:4 * g + 4],
                                             B[bk][:].rearrange(
                                                 "p (a b) -> p a b", a=4),
                                             axis=AX.X))(),
                          waits=[(c_s, c_s.n)], incs=[(c_v, 1)])
                else:
                    par8 = g
                    hs = [h for h in range(16) if h % 2 == par8]
                    for j, h in enumerate(hs):
                        base = (h % 2) * 64
                        fc = h // 2
                        sl = (cix(h) % 8) * ncj
                        pm.do("t", (lambda bk=bk, sl=sl, base=base, fc=fc,
                                    cc=cc, ncj=ncj, j=j: lambda e: e.matmul(
                                        B[bk][:, sl:sl + ncj],
                                        q_fm[base:base + 64, fc * 512 + cc * 128:
                                             fc * 512 + cc * 128 + 128],
                                        km_fm[base:base + 64,
                                              fc * 64:fc * 64 + ncj],
                                        start=(j == 0), stop=(j == 7),
                                        skip_group_check=True))(),
                              waits=(w if j == 0 else ()),
                              incs=([(c_t, 1)] if j == 7 else ()))
                    if debug:
                        pm.do("v", (lambda bk=bk, g=g, ncj=ncj: lambda e:
                                    e.reduce_sum(negm2[:, 8 * g:8 * g + 8],
                                                 B[bk][:, 0:8 * ncj].rearrange(
                                                     "p (a b) -> p a b", a=8),
                                                 axis=AX.X))(),
                              waits=[(c_t, c_t.n), (c_v, c_v.n)],
                              incs=[(c_v, 1)])
                    pm.do("s", (lambda bk=bk, ncj=ncj: lambda e:
                                e.activation(B[bk][:, 0:8 * ncj],
                                             B[bk][:, 0:8 * ncj], AF.Exp,
                                             bias=cvals[:, 5:6]))(),
                          waits=[(c_t, c_t.n), (c_v, c_v.n)], incs=[(c_s, 1)])
                    pm.do("v", (lambda bk=bk, g=g, ncj=ncj: lambda e:
                                e.reduce_sum(z2[:, 8 * g:8 * g + 8],
                                             B[bk][:, 0:8 * ncj].rearrange(
                                                 "p (a b) -> p a b", a=8),
                                             axis=AX.X))(),
                          waits=[(c_v, c_v.n), (c_s, c_s.n)], incs=[(c_v, 1)])
                    if debug:
                        for q8 in range(8):
                            pm.do("v", (lambda bk=bk, q8=q8, g=g, ncj=ncj:
                                        lambda e: e.reduce_sum(
                                            z2b[:, 8 * g + q8:8 * g + q8 + 1],
                                            B[bk][:, q8 * ncj:(q8 + 1) * ncj],
                                            axis=AX.X))(),
                                  waits=[c_v], incs=[(c_v, 1)])
                bank_free[bk] = (c_v, c_v.n)
            zred_v = c_v.n
            pm.do("s", lambda e: e.activation(lnz[:, 0:16], z1[:], AF.Ln),
                  waits=[(c_v, zred_v)], incs=[(c_s, 1)])
            pm.do("s", lambda e: e.activation(lnz[:, 16:32], z2[:], AF.Ln,
                                              scale=cvals[:, 4:5]),
                  incs=[(c_s, 1)])
            sw = [c_s.snap()]
            pm.do("v", lambda e: e.tensor_scalar(neglse[:], lnz[:], -1.0, -20.0,
                                                 ALU.mult, ALU.add),
                  waits=sw + [c_v.snap(), neglse_guard[0]], incs=[(c_v, 1)])
            return c_v.n

        def emit_lse_bounce(cc, neglse_v):
            bk = pool_next()
            pm.do("t", (lambda bk=bk: lambda e:
                        e.transpose(B[bk][0:32, 0:128], neglse[:], ident[:]))(),
                  waits=[(c_v, neglse_v), bank_free[bk]], incs=[(c_t, 1)])
            bank_gen[bk] = c_t.n
            neglse_guard[0] = (c_t, c_t.n)
            pm.do("v", (lambda bk=bk: lambda e:
                        e.tensor_copy(rows_cd[:, 0:128], B[bk][0:32, 0:128]))(),
                  waits=[(c_t, c_t.n), d_lst], incs=[(c_v, 1)])
            pm.do("v", (lambda bk=bk: lambda e:
                        e.tensor_tensor(rows_cd[:, 128:256], B[bk][0:32, 0:128],
                                        rows_cd[:, 0:128], ALU.subtract))(),
                  waits=[c_v], incs=[(c_v, 1)])
            rows_v = c_v.n
            bank_free[bk] = (c_v, rows_v)
            scr = lse_scr[cc % 2]
            pm.do("g", (lambda scr=scr: lambda e:
                        e.dma_start(out=scr, in_=rows_cd[:]))(),
                  waits=[(c_v, rows_v), lse_guard[cc % 2]], incs=[(d_lst, 16)])
            fi, fb = flat_bufs[cc % 2]
            pm.do("g", (lambda scr=scr, fi=fi: lambda e: e.dma_start(
                        out=fi.rearrange("two (h t) -> two h t", h=16),
                        in_=scr[0:16, :].rearrange("h (two t) -> two h t",
                                                   two=2)))(),
                  waits=[d_lst, flat_guard[cc % 2], hflat_guard, pq_guard],
                  incs=[(d_lsl, 16)])
            scr5 = scr[16:32, :].rearrange(
                "(half q2 q4) (two t) -> q2 two half q4 t", half=2, q2=2, two=2)
            for q2i in range(2):
                for twoi in range(2):
                    r = 2 * twoi + q2i
                    pm.do("g", (lambda fb=fb, r=r, scr5=scr5, q2i=q2i,
                                twoi=twoi: lambda e: e.dma_start(
                                    out=fb[r:r + 1, :].rearrange(
                                        "o (half q4 t) -> o half q4 t",
                                        half=2, q4=4),
                                    in_=scr5[q2i, twoi]))(),
                          incs=[(d_lsl, 16)])
            return d_lsl.n

        def emit_p2_av(cc, flat_v, part):
            c = 4 * sc + cc
            ncj = c + 1
            fi, fb = flat_bufs[cc % 2]
            o2 = ones2_sl[cc % 2]
            s4 = sel4_sl[cc % 2]
            exp_done = exp_done_st[0]
            avd = (c_t, av_done_prev[0])
            gens = ((("i", 0), ("i", 1), ("n", 0)) if part == 0 else
                    (("i", 2), ("i", 3), ("n", 1)))
            if part == 2:
                gens = ()
            for kind, g in gens:
                bk = pool_next()
                if kind == "i":
                    for qi, h in enumerate(grp_heads[g]):
                        base = (h % 2) * 64
                        fc = h // 2
                        pm.do("t", (lambda bk=bk, qi=qi, base=base, fc=fc,
                                    cc=cc: lambda e: e.matmul(
                                        B[bk][:, qi * 128:(qi + 1) * 128],
                                        k_fm[base:base + 64, fc * 512 + cc * 128:
                                             fc * 512 + cc * 128 + 128],
                                        q_fm[base:base + 64, fc * 512 + cc * 128:
                                             fc * 512 + cc * 128 + 128],
                                        start=(qi == 0), stop=False,
                                        skip_group_check=True))(),
                              waits=([bank_free[bk]] if qi == 0 else ()))
                    pm.do("t", (lambda bk=bk, g=g, fi=fi, o2=o2: lambda e:
                                e.matmul(B[bk][:], o2,
                                         fi[:, g * 512:(g + 1) * 512],
                                         start=False, stop=True,
                                         skip_group_check=True))(),
                          waits=[(d_lsl, flat_v)], incs=[(c_t, 1)])
                else:
                    par8 = g
                    pm.do("t", (lambda bk=bk, g=g, fb=fb, s4=s4: lambda e:
                                e.matmul(B[bk][:], s4,
                                         fb[:, g * 512:(g + 1) * 512],
                                         start=True, stop=False,
                                         skip_group_check=True))(),
                          waits=[bank_free[bk], (d_lsl, flat_v)])
                    hs = [h for h in range(16) if h % 2 == par8]
                    for j, h in enumerate(hs):
                        base = (h % 2) * 64
                        fc = h // 2
                        q8 = cix(h) % 8
                        sl = (q8 % 4) * 128
                        pb = 64 * (q8 // 4)
                        pm.do("t", (lambda bk=bk, sl=sl, pb=pb, base=base,
                                    fc=fc, cc=cc, ncj=ncj, j=j: lambda e: e.matmul(
                                        B[bk][pb:pb + ncj, sl:sl + 128],
                                        km_fm[base:base + 64,
                                              fc * 64:fc * 64 + ncj],
                                        q_fm[base:base + 64, fc * 512 + cc * 128:
                                             fc * 512 + cc * 128 + 128],
                                        start=False, stop=(j == 7),
                                        skip_group_check=True))(),
                              incs=([(c_t, 1)] if j == 7 else ()))
                if kind == "i":
                    pm.do("s", (lambda bk=bk, g=g: lambda e:
                                e.activation(A1T[:, g * 512:(g + 1) * 512],
                                             B[bk][:], AF.Exp))(),
                          waits=[(c_t, c_t.n), avd] + vm_latest[0],
                          incs=[(c_s, 1)])
                else:
                    pm.do("s", (lambda bk=bk, g=g: lambda e:
                                e.activation(A2T[:, g * 512:(g + 1) * 512],
                                             B[bk][:], AF.Exp))(),
                          waits=[(c_t, c_t.n), avd] + vm_latest[0],
                          incs=[(c_s, 1)])
                exp_done[(kind, g)] = c_s.n
                bank_free[bk] = (c_s, c_s.n)

            if part != 2:
                return
            first_in_bank = {}
            av_first = True
            for h in [hh for hh in range(16) if hh % 2 == 0] + \
                     [hh for hh in range(16) if hh % 2 == 1]:
                ci = cix(h)
                ec = h // 2
                abk = 6 + ec // 4
                qq = ec % 4
                pbase = (h % 2) * 64
                w = []
                if av_first:
                    w = [bank_free[6], bank_free[7]] + vm_ready
                    av_first = False
                if ci % 4 == 0:
                    w = w + [(c_s, exp_done[("i", ci // 4)])]
                stf = not first_in_bank.get((abk, pbase), False)
                pm.do("t", (lambda abk=abk, qq=qq, pbase=pbase, h=h, cc=cc,
                            ci=ci, stf=stf: lambda e: e.matmul(
                                B[abk][pbase:pbase + 64, qq * 128:(qq + 1) * 128],
                                v_tm[:, cc * 1024 + h * 64:cc * 1024 + h * 64 + 64],
                                A1T[:, ci * 128:(ci + 1) * 128],
                                start=stf, stop=False,
                                skip_group_check=True))(),
                      waits=w)
                first_in_bank[(abk, pbase)] = True
            for h in [hh for hh in range(16) if hh % 2 == 0] + \
                     [hh for hh in range(16) if hh % 2 == 1]:
                ci = cix(h)
                ec = h // 2
                abk = 6 + ec // 4
                qq = ec % 4
                pbase = (h % 2) * 64
                q8 = ci % 8
                pb2 = 64 * (q8 // 4)
                sl = (ci // 8) * 512 + (q8 % 4) * 128
                wi = []
                if h == 0:
                    wi = [(c_s, exp_done[("n", 0)])]
                elif h == 1:
                    wi = [(c_s, exp_done[("n", 1)])]
                pm.do("t", (lambda abk=abk, qq=qq, pbase=pbase, h=h, ncj=ncj,
                            sl=sl, pb2=pb2, cc=cc: lambda e: e.matmul(
                                B[abk][pbase:pbase + 64, qq * 128:(qq + 1) * 128],
                                vm_cm[pb2:pb2 + ncj, h * 64:h * 64 + 64],
                                A2T[pb2:pb2 + ncj, sl:sl + 128],
                                start=False, stop=False,
                                skip_group_check=True))(),
                      waits=wi,
                      incs=([(c_t, 1)] if h == 15 else ()))
            av_done_prev[0] = c_t.n
            flat_guard[cc % 2] = (c_t, c_t.n)
            lse_guard[cc % 2] = (c_t, c_t.n)
            for abk in (6, 7):
                pm.do("v", (lambda abk=abk, cc=cc: lambda e: e.tensor_copy(
                            attn_fm[:, 0:4096].rearrange("p (ec f) -> p ec f",
                                                         ec=8)[
                                :, (abk - 6) * 4:(abk - 6) * 4 + 4,
                                cc * 128:cc * 128 + 128],
                            B[abk][:].rearrange("p (a b) -> p a b", a=4)))(),
                      waits=[(c_t, av_done_prev[0]), attn_guard],
                      incs=[(c_v, 1)])
                bank_free[abk] = (c_v, c_v.n)

        for cc in range(4):
            emit_vsum(cc)
        vm_ready_cc = {cc: vm_latest[0] for cc in range(4)}
        nl0 = emit_p1(0)
        fl0 = emit_lse_bounce(0, nl0)
        flat_v = {0: fl0}
        for rc in range(2):
            bk = pool_next()
            w = [(c_v, xn_ready), bank_free[bk]]
            for dc in range(8):
                pm.do("t", (lambda bk=bk, rc=rc, dc=dc: lambda e: e.matmul(
                            B[bk][:],
                            U1[:, dc * 256 + rc * 128:dc * 256 + rc * 128 + 128],
                            xn_fm[:, dc * 512:(dc + 1) * 512],
                            start=(dc == 0), stop=(dc == 7)))(),
                      waits=(w if dc == 0 else ()),
                      incs=([(c_t, 1)] if dc == 7 else ()))
            bank_gen[bk] = c_t.n
            pm.do("v", (lambda bk=bk, rc=rc: lambda e:
                        e.tensor_copy(P_fm[:, rc * 512:(rc + 1) * 512], B[bk][:]))(),
                  waits=[(c_t, bank_gen[bk]), pq_guard], incs=[(c_v, 1)])
            bank_free[bk] = (c_v, c_v.n)
        p_done_sc = c_v.n
        for cc in range(4):
            if cc + 1 < 4:
                nl = emit_p1(cc + 1)
            vm_ready = vm_ready_cc[cc]
            exp_done_st[0] = {}
            emit_p2_av(cc, flat_v[cc], 0)
            if cc + 1 < 4:
                flat_v[cc + 1] = emit_lse_bounce(cc + 1, nl)
            emit_p2_av(cc, flat_v[cc], 1)
            emit_p2_av(cc, flat_v[cc], 2)
        a_guard = (c_t, av_done_prev[0])
        attn_done_v = c_v.n
        kq_guard = (c_t, c_t.n)

        if sc + 1 < nsc:
            ln_next = emit_ln(sc + 1, ln_sums_next)

        p_done = p_done_sc
        xnfm_guard = (c_t, c_t.n)

        h_gelu = [0] * 32
        qw_first = True
        for g in range(34):
            if g < 32:
                bk = 2 + g % 4
                w = [(c_v, p_done), bank_free[bk]]
                for rc in range(2):
                    pm.do("t", (lambda bk=bk, g=g, rc=rc: lambda e: e.matmul(
                                B[bk][:],
                                V1[:, rc * 4096 + g * 128:rc * 4096 + g * 128 + 128],
                                P_fm[:, rc * 512:(rc + 1) * 512],
                                start=(rc == 0), stop=(rc == 1)))(),
                          waits=(w if rc == 0 else ()),
                          incs=([(c_t, 1)] if rc == 1 else ()))
                bank_gen[bk] = c_t.n
                pm.do("s", (lambda bk=bk, g=g: lambda e: e.activation(
                            H_rot[:, (g % 4) * 512:(g % 4) * 512 + 512],
                            B[bk][:], AF.Gelu_apprx_tanh,
                            bias=b1_f[:, g:g + 1]))(),
                      waits=[(c_t, bank_gen[bk]), hrot_guard[g % 4],
                             vrow_guard[0], vrow_guard[1],
                             flat_guard[0], flat_guard[1]],
                      incs=[(c_s, 1)])
                h_gelu[g] = c_s.n
                bank_free[bk] = (c_s, c_s.n)
            gq = g - 2
            if 0 <= gq < 32:
                w = [(c_s, h_gelu[gq])]
                if qw_first:
                    w += [bank_free[6], bank_free[7], (c_v, attn_done_v)]
                for rc in range(2):
                    pm.do("t", (lambda gq=gq, rc=rc: lambda e: e.matmul(
                                B[6 + rc][:],
                                U2[:, gq * 256 + rc * 128:gq * 256 + rc * 128 + 128],
                                H_rot[:, (gq % 4) * 512:(gq % 4) * 512 + 512],
                                start=(gq == 0), stop=(gq == 31),
                                skip_group_check=True))(),
                          waits=(w if rc == 0 else ()),
                          incs=([(c_t, 1)] if (gq == 31 and rc == 1) else ()))
                if qw_first:
                    qw_first = False
                hrot_guard[gq % 4] = (c_t, c_t.n)
        q_mm_done = c_t.n
        hflat_guard = (c_t, q_mm_done)
        bank_gen[6] = bank_gen[7] = c_t.n
        for rc in range(2):
            pm.do("v", (lambda rc=rc: lambda e:
                        e.tensor_copy(Q_fm[:, rc * 512:(rc + 1) * 512],
                                      B[6 + rc][:]))(),
                  waits=[(c_t, q_mm_done)], incs=[(c_v, 1)])
            bank_free[6 + rc] = (c_v, c_v.n)
        pq_guard = (c_t, 0)

        for st in range(4):
            bks = [st % 2 * 2, st % 2 * 2 + 1]
            slot = st % 2
            for half in range(2):
                bk = bks[half]
                w = [(c_v, attn_done_v), (c_v, c_v.n), bank_free[bk]]
                for ec in range(8):
                    pm.do("t", (lambda bk=bk, st=st, half=half, ec=ec: lambda e:
                                e.matmul(
                                    B[bk][:],
                                    attn_fm[:, ec * 512 + st * 128:
                                            ec * 512 + st * 128 + 128],
                                    woT[:, ec * 1024 + half * 512:
                                        ec * 1024 + half * 512 + 512],
                                    start=(ec == 0), stop=False))(),
                          waits=(w if ec == 0 else ()))
                for rc in range(2):
                    pm.do("t", (lambda bk=bk, st=st, half=half, rc=rc: lambda e:
                                e.matmul(
                                    B[bk][:],
                                    Q_fm[:, rc * 512 + st * 128:
                                         rc * 512 + st * 128 + 128],
                                    V2[:, rc * 1024 + half * 512:
                                       rc * 1024 + half * 512 + 512],
                                    start=False, stop=False))())
                pm.do("t", (lambda bk=bk, half=half: lambda e: e.matmul(
                            B[bk][:], ones2[0:1, :],
                            bob2_row[0:1, half * 512:half * 512 + 512],
                            start=False, stop=True))(),
                      incs=[(c_t, 1)])
                bank_gen[bk] = c_t.n
                pm.do("v", (lambda bk=bk, st=st, half=half, slot=slot, xb=xb:
                            lambda e: e.tensor_tensor(
                                out_st[:, slot * 1024 + half * 512:
                                       slot * 1024 + half * 512 + 512],
                                B[bk][:],
                                xb[:, st * 1024 + half * 512:
                                   st * 1024 + half * 512 + 512],
                                ALU.add))(),
                      waits=[(c_t, bank_gen[bk]), stage_guard[slot]],
                      incs=[(c_v, 1)])
                bank_free[bk] = (c_v, c_v.n)
            ev = c_v.n
            pm.do("y", (lambda sc=sc, st=st, slot=slot: lambda e: e.dma_start(
                        out=out_h[sc * 512 + st * 128:sc * 512 + st * 128 + 128, :],
                        in_=out_st[:, slot * 1024:slot * 1024 + 1024]))(),
                  waits=[(c_v, ev)], incs=[(d_op[slot], 16)])
            stage_guard[slot] = (d_op[slot], d_op[slot].n)
        x_guard[par] = (c_v, c_v.n)
        pq_guard = (c_t, c_t.n)
        attn_guard = (c_v, c_v.n)


    if debug:
        d_dbg = Ctr(nc, "d_dbg")
        dbg_list = [("dbg_xnfm", xn_fm), ("dbg_q", q_fm), ("dbg_k", k_fm),
                    ("dbg_v", v_tm), ("dbg_km", km_fm), ("dbg_vm", vm_cm),
                    ("dbg_attn", attn_fm), ("dbg_P", P_fm), ("dbg_Q", Q_fm),
                    ("dbg_negm1", negm1), ("dbg_negm2", negm2),
                    ("dbg_z1", z1), ("dbg_z2", z2), ("dbg_z2b", z2b), ("dbg_neglse", neglse),
                    ("dbg_A1T", A1T), ("dbg_A2T", A2T), ("dbg_rowscd", rows_cd),
                    ("dbg_bqeff", bq_eff), ("dbg_bkf", bk_f), ("dbg_b1f", b1_f),
                    ("dbg_bob2", bob2_row), ("dbg_bvrow", bv_row),
                    ("dbg_wqT", wqT), ("dbg_U1", U1)]
        allw = [c_t, c_v, c_s]
        for (nm, tt) in dbg_list:
            od = ap(nc.dram_tensor(nm, list(tt.shape), tt.dtype,
                                   kind="ExternalOutput"))
            pm.do("g", (lambda od=od, tt=tt: lambda e:
                        e.dma_start(out=od, in_=tt[:]))(),
                  waits=[c_t, c_v, c_s], incs=[(d_dbg, 16)])

    fin_w = [d_op[0], d_op[1], d_lst, d_lsl, d_vm[0], d_vm[1]]
    if debug:
        pm.do("g", lambda e: e.nop(), waits=[d_dbg])
    pm.do("y", lambda e: e.nop(), waits=fin_w)

    with nc.Block("main") as blk2:
        pm.emit(blk2)

    return nc


_NC_CACHE = {}


def kernel(**inputs):
    nsc = NSC_FULL
    if nsc not in _NC_CACHE:
        _NC_CACHE[nsc] = build_nc(nsc)
    nc = _NC_CACHE[nsc]
    x = np.ascontiguousarray(inputs["L0_attn_hidden_states"])
    weights = {k: np.ascontiguousarray(v) for k, v in inputs.items()
               if k != "L0_attn_hidden_states"}
    in_maps = []
    for core in range(NCORE):
        m = dict(weights)
        m["L0_attn_hidden_states"] = x[core]
        in_maps.append(m)
    res = run_bass_kernel_spmd(nc, in_maps, core_ids=list(range(NCORE)))
    return np.stack([res.results[i]["out"] for i in range(NCORE)], axis=0)


# version 14
# speedup vs baseline: 1.1895x; 1.1895x over previous
import numpy as np
import ml_dtypes
import concourse.bass as bass
import concourse.mybir as mybir
from concourse.bass_utils import run_bass_kernel_spmd

F32 = mybir.dt.float32
BF16 = mybir.dt.bfloat16
AF = mybir.ActivationFunctionType
ALU = mybir.AluOpType
AX = mybir.AxisListType
BF = ml_dtypes.bfloat16

NCORE = 8
S = 8192
E = 1024
NH = 16
D = 64
T = 128
NSC_FULL = 16
SCT = 512
DFF = 4096
RK = 256


class Ctr:

    def __init__(self, nc, name):
        self.h = nc.alloc_semaphore(name)
        self.n = 0

    def snap(self):
        return (self, self.n)


class Prog:

    ENGINES = ("t", "v", "s", "y", "g")

    def __init__(self):
        self.steps = []

    def do(self, eng, fn, waits=(), incs=()):
        w = [x if isinstance(x, tuple) else x.snap() for x in waits]
        self.steps.append((eng, w, fn, list(incs)))
        for (c, k) in incs:
            c.n += k

    def emit(self, block):
        def mk(eng_key):
            def run(engine):
                done = {}
                for (ek, waits, fn, incs) in self.steps:
                    if ek != eng_key:
                        continue
                    for (c, val) in waits:
                        if val > 0 and done.get(id(c), -1) < val:
                            engine.wait_ge(c.h, val)
                            done[id(c)] = val
                    instr = fn(engine)
                    for (c, k) in incs:
                        instr.then_inc(c.h, k)
            return run

        block.tensor(mk("t"))
        block.vector(mk("v"))
        block.scalar(mk("s"))
        block.sync(mk("y"))
        block.gpsimd(mk("g"))


def build_nc(nsc=NSC_FULL, debug=False):
    nc = bass.Bass("TRN2", target_bir_lowering=False)
    ap = lambda t: t.ap()

    x_h = ap(nc.dram_tensor("L0_attn_hidden_states", [S, E], F32, kind="ExternalInput"))
    wq_h = ap(nc.dram_tensor("L0_attn_wq", [E, E], F32, kind="ExternalInput"))
    bq_h = ap(nc.dram_tensor("L0_attn_bq", [E], F32, kind="ExternalInput"))
    wk_h = ap(nc.dram_tensor("L0_attn_wk", [E, E], F32, kind="ExternalInput"))
    bk_h = ap(nc.dram_tensor("L0_attn_bk", [E], F32, kind="ExternalInput"))
    wv_h = ap(nc.dram_tensor("L0_attn_wv", [E, E], F32, kind="ExternalInput"))
    bv_h = ap(nc.dram_tensor("L0_attn_bv", [E], F32, kind="ExternalInput"))
    wo_h = ap(nc.dram_tensor("L0_attn_wo", [E, E], F32, kind="ExternalInput"))
    bo_h = ap(nc.dram_tensor("L0_attn_bo", [E], F32, kind="ExternalInput"))
    u1_h = ap(nc.dram_tensor("L0_ffn_U1", [E, RK], F32, kind="ExternalInput"))
    v1_h = ap(nc.dram_tensor("L0_ffn_V1", [RK, DFF], F32, kind="ExternalInput"))
    b1_h = ap(nc.dram_tensor("L0_ffn_b1", [DFF], F32, kind="ExternalInput"))
    u2_h = ap(nc.dram_tensor("L0_ffn_U2", [DFF, RK], F32, kind="ExternalInput"))
    v2_h = ap(nc.dram_tensor("L0_ffn_V2", [RK, E], F32, kind="ExternalInput"))
    b2_h = ap(nc.dram_tensor("L0_ffn_b2", [E], F32, kind="ExternalInput"))
    out_h = ap(nc.dram_tensor("out", [S, E], F32, kind="ExternalOutput"))
    lse_scr = [ap(nc.dram_tensor(f"lse_scr{i}", [32, 256], BF16, kind="Internal"))
               for i in range(2)]

    ident_c = nc.inline_tensor(np.eye(128, dtype=np.float32), name="c_ident")
    identb_c = nc.inline_tensor(np.eye(128, dtype=BF), name="c_identb")
    o2d = np.zeros((34, 128), dtype=BF)
    o2d[0, :] = 1
    o2d[1, :] = 1
    o2d[32, :] = 1
    o2d[33, :] = 1
    ones2_c = nc.inline_tensor(o2d, name="c_ones2")
    sel4 = np.zeros((68, 128), dtype=BF)
    for b0 in (0, 64):
        sel4[b0 + 0, 0:64] = 1
        sel4[b0 + 1, 64:128] = 1
        sel4[b0 + 2, 0:64] = 1
        sel4[b0 + 3, 64:128] = 1
    sel4_c = nc.inline_tensor(sel4, name="c_sel4")
    onescol_c = nc.inline_tensor(np.ones((128, 1), dtype=BF), name="c_onescol")
    cv = np.zeros((128, 8), dtype=np.float32)
    cv[:, 0] = 1e-5
    cv[:, 1] = -0.5
    cv[:, 2] = 0.125
    cv[:, 3] = 1.0 / 128
    cv[:, 4] = 128.0
    cv[:, 5] = -20.0
    cv[:, 6] = 1.0 / 1024
    cvals_c = nc.inline_tensor(cv, name="c_cvals")

    sb = nc.alloc_sbuf_tensor
    wqT = sb("wqT", [128, 8192], BF16)
    wkT = sb("wkT", [128, 8192], BF16)
    wvT = sb("wvT", [128, 8192], BF16)
    woT = sb("woT", [128, 8192], BF16)
    U1 = sb("U1", [128, 2048], BF16)
    V1 = sb("V1", [128, 8192], BF16)
    U2 = sb("U2", [128, 8192], BF16)
    V2 = sb("V2", [128, 2048], BF16)
    x_buf = sb("x_buf", [128, 8192], F32)
    out_st = sb("out_st", [128, 2048], F32)
    xn_fm = sb("xn_fm", [128, 4096], BF16)
    q_fm = sb("q_fm", [128, 4096], BF16)
    k_fm = sb("k_fm", [128, 4096], BF16)
    v_tm = sb("v_tm", [128, 4096], BF16)
    attn_fm = sb("attn_fm", [128, 4096], BF16)
    xn_tm = v_tm
    A1T = sb("A1T", [128, 2048], BF16)
    A2T = sb("A2T", [128, 1024], BF16)
    H_rot = sb("H_rot", [128, 2048], BF16)
    P_fm = sb("P_fm", [128, 1024], BF16)
    Q_fm = sb("Q_fm", [128, 1024], BF16)
    km_fm = sb("km_fm", [128, 512], BF16)
    vm_cm = sb("vm_cm", [128, 1024], BF16)
    ident = sb("identity", [128, 128], F32)
    ident_b = sb("identity_b", [128, 128], BF16)
    ones2 = sb("ones2", [34, 128], BF16)
    sel4_t = sb("sel4", [68, 128], BF16)
    ones_col = sb("ones_col", [128, 1], BF16)
    cvals = sb("cvals", [128, 8], F32)
    bq_eff = sb("bq_eff", [128, 8], F32)
    bk_f = sb("bk_f", [128, 8], F32)
    b1_f = sb("b1_f", [128, 32], F32)
    bv_row = sb("bv_row", [1, 1024], BF16)
    bob2_row = sb("bob2_row", [1, 1024], BF16)
    stats = sb("stats", [128, 16], F32)
    negm1 = sb("negm1", [128, 16], F32)
    negm2 = sb("negm2", [128, 16], F32)
    z1 = sb("z1", [128, 16], F32)
    z2 = sb("z2", [128, 16], F32)
    z2b = negm1
    lnz = sb("lnz", [128, 32], F32)
    neglse = sb("neglse", [128, 32], F32)
    rows_cd = sb("rows_cd", [32, 256], BF16)
    kcol = sb("kcol", [128, 1], F32)
    bias_r = out_st[0:32, 0:128]
    flat_bufs = {0: (H_rot[0:2, 0:2048], H_rot[64:68, 0:1024]),
                 1: (H_rot[32:34, 0:2048], Q_fm[0:4, 0:1024])}
    vrow_slots = None

    c_t = Ctr(nc, "c_t")
    c_v = Ctr(nc, "c_v")
    c_s = Ctr(nc, "c_s")
    d_xp = [Ctr(nc, "d_x0"), Ctr(nc, "d_x1")]
    d_cst = Ctr(nc, "d_cst")
    d_ffw = Ctr(nc, "d_ffw")
    d_st = [Ctr(nc, "d_st0"), Ctr(nc, "d_st1")]
    d_lst = Ctr(nc, "d_lst")
    d_lsl = Ctr(nc, "d_lsl")
    d_vm = [Ctr(nc, "d_vm0"), Ctr(nc, "d_vm1")]
    d_op = [Ctr(nc, "d_o0"), Ctr(nc, "d_o1")]

    ps = Prog()
    if debug:
        mlist = (km_fm, vm_cm, A1T, A2T, H_rot, P_fm, Q_fm, attn_fm,
                 q_fm, k_fm, v_tm, xn_fm, negm1, negm2, z1, z2, lnz,
                 neglse, rows_cd, stats, out_st, x_buf, kcol)
        for i, tt in enumerate(mlist):
            ps.do("v", (lambda tt=tt: lambda e: e.memset(tt[:], 0.0))(),
                  incs=([(c_v, 1)] if i == len(mlist) - 1 else ()))

    def dma(eng_attr, out, in_):
        return lambda e: e.dma_start(out=out, in_=in_)

    for (dst, src) in ((ident, ident_c), (ident_b, identb_c), (ones2, ones2_c),
                       (sel4_t, sel4_c), (ones_col, onescol_c), (cvals, cvals_c)):
        ps.do("y", dma("y", dst[:], src.ap()), incs=[(d_cst, 16)])
    xr = x_h.rearrange("(sc st p) f -> sc p st f", p=128, st=4)
    ps.do("y", dma("y", x_buf[:, 0:4096].rearrange("p (st f) -> p st f", st=4),
                   xr[0]), waits=[c_v], incs=[(d_xp[0], 16)])

    ps.do("g", dma("g", U1[:].rearrange("p (dc r) -> p dc r", dc=8),
                   u1_h.rearrange("(dc p) r -> p dc r", p=128)),
          waits=[c_v], incs=[(d_ffw, 16)])
    ps.do("g", dma("g", V1[:].rearrange("p (rc f) -> p rc f", rc=2),
                   v1_h.rearrange("(rc p) f -> p rc f", p=128)), incs=[(d_ffw, 16)])
    ps.do("g", dma("g", U2[:].rearrange("p (gc r) -> p gc r", gc=32),
                   u2_h.rearrange("(gc p) r -> p gc r", p=128)), incs=[(d_ffw, 16)])
    ps.do("g", dma("g", V2[:].rearrange("p (rc d) -> p rc d", rc=2),
                   v2_h.rearrange("(rc p) d -> p rc d", p=128)), incs=[(d_ffw, 16)])
    ps.do("g", dma("g", bias_r[0:8, :], bq_h.rearrange("(a t) -> a t", t=128)),
          incs=[(d_ffw, 16)])

    ps.do("g", dma("g", bv_row[:], bv_h.rearrange("(a t) -> a t", t=1024)),
          incs=[(d_ffw, 16)])
    ps.do("g", dma("g", out_st[64:65, 0:1024], bo_h.rearrange("(a t) -> a t", t=1024)),
          incs=[(d_ffw, 16)])
    ps.do("g", dma("g", out_st[64:65, 1024:2048], b2_h.rearrange("(a t) -> a t", t=1024)),
          incs=[(d_ffw, 16)])

    ps.do("v", lambda e: e.tensor_tensor(bob2_row[:], out_st[64:65, 0:1024],
                                         out_st[64:65, 1024:2048], ALU.add),
          waits=[d_ffw], incs=[(c_v, 1)])

    pb = nc.alloc_psum_tensor
    B = [pb(f"B{i}", [128, 512], F32) for i in range(8)]

    for bi in range(8):
        ps.do("v", (lambda bi=bi: lambda e: e.memset(B[bi][:], 0.0))(),
              incs=([(c_v, 1)] if bi == 7 else ()))
    dbg_v0 = c_v.n
    wlist = [(wq_h, wqT, 2), (wk_h, wkT, None), (wv_h, wvT, None),
             (wo_h, woT, None)]
    stage = H_rot
    ev_after = {}
    for wi, (wsrc, dstT, scale_col) in enumerate(wlist):
        for fc in range(8):
            slot = (wi * 8 + fc) % 2
            stg = stage[:, slot * 1024:(slot + 1) * 1024]
            w_st = [(c_t, ev_after.get(slot, 0)), (c_v, dbg_v0)]
            ps.do("g", dma("g", stg, wsrc[fc * 128:(fc + 1) * 128, :]),
                  waits=w_st, incs=[(d_st[slot], 16)])
            wts = [(d_st[slot], d_st[slot].n), (d_cst, d_cst.n),
                   (c_v, ev_after.get(("e", slot), dbg_v0))]
            for half in range(2):
                bank = B[slot * 2 + half]
                for eci in range(4):
                    ec = half * 4 + eci
                    ps.do("t", (lambda b=bank, sg=stg, ec=ec, eci=eci: lambda e:
                                e.matmul(b[:, eci * 128:(eci + 1) * 128],
                                         sg[:, ec * 128:(ec + 1) * 128],
                                         ident_b[:], start=(eci == 0),
                                         stop=(eci == 3)))(),
                          waits=(wts if (half == 0 and eci == 0) else ()),
                          incs=([(c_t, 1)] if (half == 1 and eci == 3) else ()))
            ev_after[slot] = c_t.n
            def mkev(slot=slot, dstT=dstT, fc=fc, scale_col=scale_col):
                def f(e):
                    o = dstT[:, 0:8192].rearrange("p (ec f) -> p ec f", ec=8)[
                        :, :, fc * 128:(fc + 1) * 128]
                    i0 = B[slot * 2][:].rearrange("p (a b) -> p a b", a=4)
                    i1 = B[slot * 2 + 1][:].rearrange("p (a b) -> p a b", a=4)
                    if scale_col is not None:
                        e.tensor_scalar(o[:, 0:4], i0,
                                        cvals[:, scale_col:scale_col + 1],
                                        None, ALU.mult)
                        return e.tensor_scalar(o[:, 4:8], i1,
                                               cvals[:, scale_col:scale_col + 1],
                                               None, ALU.mult)
                    e.tensor_copy(o[:, 0:4], i0)
                    return e.tensor_copy(o[:, 4:8], i1)
                return f
            ps.do("v", mkev(), waits=[c_t], incs=[(c_v, 1)])
            ev_after[("e", slot)] = c_v.n

    with nc.Block("setup") as blk1:
        ps.emit(blk1)

    setup_t, setup_v, setup_s = c_t.n, c_v.n, c_s.n

    pm = Prog()

    pm.do("t", lambda e: e.transpose(B[0][0:128, 0:8], bias_r[0:8, :], ident[0:8, 0:8]),
          waits=[(c_v, setup_v)], incs=[(c_t, 1)])
    t_bias0 = c_t.n
    pm.do("g", dma("g", bias_r[0:8, :], bk_h.rearrange("(a t) -> a t", t=128)),
          waits=[(c_t, t_bias0)], incs=[(d_ffw, 16)])
    pm.do("t", lambda e: e.transpose(B[0][0:128, 8:16], bias_r[0:8, :], ident[0:8, 0:8]),
          waits=[(d_ffw, d_ffw.n)], incs=[(c_t, 1)])
    t_bias1 = c_t.n
    pm.do("v", lambda e: e.tensor_scalar(bq_eff[:], B[0][:, 0:8],
                                         cvals[:, 2:3], None, ALU.mult),
          waits=[(c_t, t_bias1)], incs=[(c_v, 1)])
    pm.do("v", lambda e: e.tensor_copy(bk_f[:], B[0][:, 8:16]), incs=[(c_v, 1)])
    v_bias1 = c_v.n
    pm.do("g", dma("g", bias_r[:], b1_h.rearrange("(a t) -> a t", t=128)),
          waits=[(c_v, v_bias1)], incs=[(d_ffw, 16)])
    pm.do("t", lambda e: e.transpose(B[0][0:128, 16:48], bias_r[:], ident[0:32, 0:32]),
          waits=[(d_ffw, d_ffw.n), (c_v, v_bias1)], incs=[(c_t, 1)])
    pm.do("v", lambda e: e.tensor_copy(b1_f[:], B[0][:, 16:48]),
          waits=[c_t], incs=[(c_v, 1)])

    bank_gen = [c_t.snap()[1]] * 8
    bank_free = [(c_v, 0)] * 8

    def after_all(*cs):
        return [c.snap() for c in cs]

    x_guard = {0: (c_v, 0), 1: (c_v, 0)}
    lse_guard = {0: (c_t, 0), 1: (c_t, 0)}
    flat_guard = {0: (c_t, 0), 1: (c_t, 0)}
    vrow_guard = {0: (d_vm[0], 0), 1: (d_vm[1], 0)}
    vm_ready = [(d_vm[0], 0), (d_vm[1], 0)]
    a_guard = (c_t, 0)
    stage_guard = {0: (d_op[0], 0), 1: (d_op[1], 0)}
    kq_guard = (c_t, 0)
    xnfm_guard = (c_t, 0)
    ones2_sl = {0: ones2[0:2, :], 1: ones2[32:34, :]}
    sel4_sl = {0: sel4_t[64:68, :], 1: sel4_t[0:4, :]}
    vrow_slots = {0: A1T[0:1, 0:1024], 1: A2T[0:1, 0:1024]}
    av_done_prev = [0]
    exp_done_st = [{}]
    neglse_guard = [(c_t, 0)]
    attn_guard = (c_v, 0)
    hrot_guard = [(c_t, 0)] * 4
    hflat_guard = (c_t, 0)
    pq_guard = (c_t, 0)

    def emit_ln_sums(scx):
        parx = scx % 2
        xbx = x_buf[:, parx * 4096:(parx + 1) * 4096]
        for st in range(4):
            xt = xbx[:, st * 1024:(st + 1) * 1024]
            s_sum = stats[:, st * 4:st * 4 + 1]
            s_nm = stats[:, st * 4 + 2:st * 4 + 3]
            pm.do("v", (lambda xt=xt, s_sum=s_sum: lambda e:
                        e.reduce_sum(s_sum, xt, axis=AX.X))(),
                  waits=[d_xp[parx], x_guard[parx], c_v.snap()], incs=[(c_v, 1)])
            pm.do("v", (lambda s_sum=s_sum, s_nm=s_nm: lambda e:
                        e.tensor_scalar(s_nm, s_sum, -1.0 / 1024, None,
                                        ALU.mult))(),
                  waits=[c_v], incs=[(c_v, 1)])
        return c_v.n

    def emit_ln(scx, sums_v):
        parx = scx % 2
        xbx = x_buf[:, parx * 4096:(parx + 1) * 4096]
        ln_done = []
        for st in range(4):
            xt = xbx[:, st * 1024:(st + 1) * 1024]
            xnt = xn_tm[:, st * 1024:(st + 1) * 1024]
            s_sq = stats[:, st * 4 + 1:st * 4 + 2]
            s_nm = stats[:, st * 4 + 2:st * 4 + 3]
            s_rs = stats[:, st * 4 + 3:st * 4 + 4]
            pm.do("s", (lambda xt=xt, xnt=xnt, s_sq=s_sq, s_nm=s_nm: lambda e:
                        e.activation(xnt, xt, AF.Square, bias=s_nm,
                                     accum_out=s_sq))(),
                  waits=[(c_v, sums_v), kq_guard, c_s.snap()], incs=[(c_s, 1)])
            pm.do("s", (lambda s_sq=s_sq, s_rs=s_rs: lambda e:
                        e.activation(s_rs, s_sq, AF.Ln, bias=cvals[:, 0:1],
                                     scale=cvals[:, 6:7]))(),
                  waits=[c_s], incs=[(c_s, 1)])
            pm.do("s", (lambda s_rs=s_rs: lambda e:
                        e.activation(s_rs, s_rs, AF.Exp, scale=cvals[:, 1:2]))(),
                  waits=[c_s], incs=[(c_s, 1)])
            sw2 = [c_s.snap()]
            pm.do("v", (lambda xt=xt, xnt=xnt, s_nm=s_nm, s_rs=s_rs: lambda e:
                        e.tensor_scalar(xnt, xt, s_nm, s_rs, ALU.add,
                                        ALU.mult))(),
                  waits=sw2 + [c_v.snap()], incs=[(c_v, 1)])
            ln_done.append(c_v.n)
        return ln_done


    cix = lambda h: (h % 2) * 8 + h // 2
    grp_heads = [[h for h in range(16) if cix(h) // 4 == g] for g in range(4)]
    pool_i = [0]

    def pool_next():
        bk = pool_i[0]
        pool_i[0] = (bk + 1) % 6
        return bk

    def emit_xnt(ln_done_x):
        for st in range(4):
            for half in range(2):
                bk = pool_next()
                w = [(c_v, ln_done_x[st]), bank_free[bk]]
                for eci in range(4):
                    ec = half * 4 + eci
                    pm.do("t", (lambda bk=bk, st=st, ec=ec, eci=eci: lambda e:
                                e.matmul(B[bk][:, eci * 128:(eci + 1) * 128],
                                         xn_tm[:, st * 1024 + ec * 128:
                                               st * 1024 + ec * 128 + 128],
                                         ident_b[:], start=(eci == 0),
                                         stop=(eci == 3)))(),
                          waits=(w if eci == 0 else ()),
                          incs=([(c_t, 1)] if eci == 3 else ()))
                bank_gen[bk] = c_t.n
                pm.do("v", (lambda bk=bk, st=st, half=half: lambda e:
                            e.tensor_copy(
                                xn_fm[:, 0:4096].rearrange(
                                    "p (ec f) -> p ec f", ec=8)[
                                    :, half * 4:half * 4 + 4,
                                    st * 128:st * 128 + 128],
                                B[bk][:].rearrange("p (a b) -> p a b", a=4)))(),
                      waits=[(c_t, bank_gen[bk]), xnfm_guard], incs=[(c_v, 1)])
                bank_free[bk] = (c_v, c_v.n)
        return c_v.n

    ln_next = emit_ln(0, emit_ln_sums(0))
    xn_ready_next = emit_xnt(ln_next)

    for sc in range(nsc):
        par = sc % 2
        xb = x_buf[:, par * 4096:(par + 1) * 4096]

        if sc + 1 < nsc:
            npar = (sc + 1) % 2
            pm.do("y", (lambda sc=sc, npar=npar: lambda e: e.dma_start(
                        out=x_buf[:, npar * 4096:(npar + 1) * 4096].rearrange(
                            "p (st f) -> p st f", st=4),
                        in_=xr[sc + 1]))(),
                  waits=[x_guard[npar]], incs=[(d_xp[npar], 16)])

        ln_done = ln_next
        xn_ready = xn_ready_next

        for (wT, dst, brow) in ((wqT, q_fm, bq_eff), (wkT, k_fm, bk_f)):
            for fc in range(8):
                bk = fc % 4
                w = [(c_v, xn_ready), bank_free[bk], (c_s, setup_s)]
                for ec in range(8):
                    pm.do("t", (lambda bk=bk, wT=wT, fc=fc, ec=ec: lambda e:
                                e.matmul(B[bk][:],
                                         wT[:, ec * 1024 + fc * 128:
                                            ec * 1024 + fc * 128 + 128],
                                         xn_fm[:, ec * 512:(ec + 1) * 512],
                                         start=(ec == 0), stop=(ec == 7)))(),
                          waits=(w if ec == 0 else ()),
                          incs=([(c_t, 1)] if ec == 7 else ()))
                bank_gen[bk] = c_t.n
                pm.do("s", (lambda bk=bk, dst=dst, fc=fc, brow=brow: lambda e:
                            e.activation(dst[:, fc * 512:(fc + 1) * 512], B[bk][:],
                                         AF.Identity, bias=brow[:, fc:fc + 1]))(),
                      waits=[(c_t, bank_gen[bk]), kq_guard], incs=[(c_s, 1)])
                bank_free[bk] = (c_s, c_s.n)
        for st in range(4):
            for half in range(2):
                bk = (st * 2 + half) % 4
                w = [(c_v, xn_ready), bank_free[bk]]
                for ec in range(8):
                    pm.do("t", (lambda bk=bk, st=st, half=half, ec=ec: lambda e:
                                e.matmul(B[bk][:],
                                         xn_fm[:, ec * 512 + st * 128:
                                               ec * 512 + st * 128 + 128],
                                         wvT[:, ec * 1024 + half * 512:
                                             ec * 1024 + half * 512 + 512],
                                         start=(ec == 0), stop=False))(),
                          waits=(w if ec == 0 else ()))
                pm.do("t", (lambda bk=bk, half=half: lambda e:
                            e.matmul(B[bk][:], ones2[0:1, :],
                                     bv_row[0:1, half * 512:half * 512 + 512],
                                     start=False, stop=True))(),
                      incs=[(c_t, 1)])
                bank_gen[bk] = c_t.n
                pm.do("v", (lambda bk=bk, st=st, half=half: lambda e:
                            e.tensor_copy(v_tm[:, st * 1024 + half * 512:
                                               st * 1024 + half * 512 + 512],
                                          B[bk][:]))(),
                      waits=[(c_t, bank_gen[bk]), kq_guard], incs=[(c_v, 1)])
                bank_free[bk] = (c_v, c_v.n)
        v_done = c_v.n
        q_done_s = c_s.n

        if sc + 1 < nsc:
            ln_sums_next = emit_ln_sums(sc + 1)

        km_done = []
        for cc in range(4):
            c = 4 * sc + cc
            for fc in range(8):
                pm.do("v", (lambda fc=fc, cc=cc: lambda e:
                            e.reduce_sum(kcol[:], k_fm[:, fc * 512 + cc * 128:
                                                       fc * 512 + cc * 128 + 128],
                                         axis=AX.X))(),
                      waits=[(c_s, q_done_s), (c_v, c_v.n)], incs=[(c_v, 1)])
                pm.do("v", (lambda fc=fc, c=c: lambda e:
                            e.tensor_scalar(km_fm[:, fc * 64 + c:fc * 64 + c + 1],
                                            kcol[:], 1.0 / 128, None, ALU.mult))(),
                      waits=[c_v], incs=[(c_v, 1)])
            km_done.append(c_v.n)

        vm_latest = [[(d_vm[0], 0), (d_vm[1], 0)]]

        def emit_vsum(cc):
            c = 4 * sc + cc
            vr = vrow_slots[cc % 2]
            w = [(c_v, v_done), bank_free[6], bank_free[7], vrow_guard[cc % 2],
                 hflat_guard, a_guard]
            pm.do("t", (lambda cc=cc: lambda e:
                        e.matmul(B[6][0:1, :], ones_col[:],
                                 v_tm[:, cc * 1024:cc * 1024 + 512],
                                 start=True, stop=True))(),
                  waits=w, incs=[(c_t, 1)])
            pm.do("t", (lambda cc=cc: lambda e:
                        e.matmul(B[7][0:1, :], ones_col[:],
                                 v_tm[:, cc * 1024 + 512:cc * 1024 + 1024],
                                 start=True, stop=True))(),
                  incs=[(c_t, 1)])
            bank_gen[6] = bank_gen[7] = c_t.n
            pm.do("s", (lambda vr=vr: lambda e:
                        e.activation(vr[0:1, 0:512], B[6][0:1, :], AF.Copy))(),
                  waits=[(c_t, c_t.n)], incs=[(c_s, 1)])
            pm.do("s", (lambda vr=vr: lambda e:
                        e.activation(vr[0:1, 512:1024], B[7][0:1, :], AF.Copy))(),
                  incs=[(c_s, 1)])
            bank_free[6] = bank_free[7] = (c_s, c_s.n)
            sw = [c_s.snap()]
            pm.do("g", (lambda c=c, vr=vr: lambda e:
                        e.dma_start(out=vm_cm[c:c + 1, :], in_=vr))(),
                  waits=sw, incs=[(d_vm[cc % 2], 16)])
            pm.do("g", (lambda c=c, vr=vr: lambda e:
                        e.dma_start(out=vm_cm[64 + c:65 + c, :], in_=vr))(),
                  incs=[(d_vm[cc % 2], 16)])
            vrow_guard[cc % 2] = (d_vm[cc % 2], d_vm[cc % 2].n)
            vm_latest[0] = [(d_vm[0], d_vm[0].n), (d_vm[1], d_vm[1].n)]
            return vm_latest[0]


        def emit_p1(cc):
            c = 4 * sc + cc
            ncj = c + 1
            first = [(c_s, q_done_s), (c_v, km_done[cc]), a_guard]
            for kind, g in (("i", 0), ("i", 1), ("n", 0), ("i", 2), ("i", 3),
                            ("n", 1)):
                bk = pool_next()
                w = [bank_free[bk]] + first
                first = []
                if kind == "i":
                    for qi, h in enumerate(grp_heads[g]):
                        base = (h % 2) * 64
                        fc = h // 2
                        pm.do("t", (lambda bk=bk, qi=qi, base=base, fc=fc,
                                    cc=cc: lambda e: e.matmul(
                                        B[bk][:, qi * 128:(qi + 1) * 128],
                                        q_fm[base:base + 64, fc * 512 + cc * 128:
                                             fc * 512 + cc * 128 + 128],
                                        k_fm[base:base + 64, fc * 512 + cc * 128:
                                             fc * 512 + cc * 128 + 128],
                                        start=(qi == 0), stop=(qi == 3),
                                        skip_group_check=True))(),
                              waits=(w if qi == 0 else ()),
                              incs=([(c_t, 1)] if qi == 3 else ()))
                    pm.do("s", (lambda bk=bk: lambda e:
                                e.activation(B[bk][:], B[bk][:], AF.Exp,
                                             bias=cvals[:, 5:6]))(),
                          waits=[(c_t, c_t.n)], incs=[(c_s, 1)])
                    pm.do("v", (lambda bk=bk, g=g: lambda e:
                                e.reduce_sum(z1[:, 4 * g:4 * g + 4],
                                             B[bk][:].rearrange(
                                                 "p (a b) -> p a b", a=4),
                                             axis=AX.X))(),
                          waits=[(c_s, c_s.n)], incs=[(c_v, 1)])
                else:
                    par8 = g
                    hs = [h for h in range(16) if h % 2 == par8]
                    for j, h in enumerate(hs):
                        base = (h % 2) * 64
                        fc = h // 2
                        sl = (cix(h) % 8) * ncj
                        pm.do("t", (lambda bk=bk, sl=sl, base=base, fc=fc,
                                    cc=cc, ncj=ncj, j=j: lambda e: e.matmul(
                                        B[bk][:, sl:sl + ncj],
                                        q_fm[base:base + 64, fc * 512 + cc * 128:
                                             fc * 512 + cc * 128 + 128],
                                        km_fm[base:base + 64,
                                              fc * 64:fc * 64 + ncj],
                                        start=(j == 0), stop=(j == 7),
                                        skip_group_check=True))(),
                              waits=(w if j == 0 else ()),
                              incs=([(c_t, 1)] if j == 7 else ()))
                    if debug:
                        pm.do("v", (lambda bk=bk, g=g, ncj=ncj: lambda e:
                                    e.reduce_sum(negm2[:, 8 * g:8 * g + 8],
                                                 B[bk][:, 0:8 * ncj].rearrange(
                                                     "p (a b) -> p a b", a=8),
                                                 axis=AX.X))(),
                              waits=[(c_t, c_t.n), (c_v, c_v.n)],
                              incs=[(c_v, 1)])
                    pm.do("s", (lambda bk=bk, ncj=ncj: lambda e:
                                e.activation(B[bk][:, 0:8 * ncj],
                                             B[bk][:, 0:8 * ncj], AF.Exp,
                                             bias=cvals[:, 5:6]))(),
                          waits=[(c_t, c_t.n), (c_v, c_v.n)], incs=[(c_s, 1)])
                    pm.do("v", (lambda bk=bk, g=g, ncj=ncj: lambda e:
                                e.reduce_sum(z2[:, 8 * g:8 * g + 8],
                                             B[bk][:, 0:8 * ncj].rearrange(
                                                 "p (a b) -> p a b", a=8),
                                             axis=AX.X))(),
                          waits=[(c_v, c_v.n), (c_s, c_s.n)], incs=[(c_v, 1)])
                    if debug:
                        for q8 in range(8):
                            pm.do("v", (lambda bk=bk, q8=q8, g=g, ncj=ncj:
                                        lambda e: e.reduce_sum(
                                            z2b[:, 8 * g + q8:8 * g + q8 + 1],
                                            B[bk][:, q8 * ncj:(q8 + 1) * ncj],
                                            axis=AX.X))(),
                                  waits=[c_v], incs=[(c_v, 1)])
                bank_free[bk] = (c_v, c_v.n)
            zred_v = c_v.n
            pm.do("s", lambda e: e.activation(lnz[:, 0:16], z1[:], AF.Ln),
                  waits=[(c_v, zred_v)], incs=[(c_s, 1)])
            pm.do("s", lambda e: e.activation(lnz[:, 16:32], z2[:], AF.Ln,
                                              scale=cvals[:, 4:5]),
                  incs=[(c_s, 1)])
            sw = [c_s.snap()]
            pm.do("v", lambda e: e.tensor_scalar(neglse[:], lnz[:], -1.0, -20.0,
                                                 ALU.mult, ALU.add),
                  waits=sw + [c_v.snap(), neglse_guard[0]], incs=[(c_v, 1)])
            return c_v.n

        def emit_lse_bounce(cc, neglse_v):
            bk = pool_next()
            pm.do("t", (lambda bk=bk: lambda e:
                        e.transpose(B[bk][0:32, 0:128], neglse[:], ident[:]))(),
                  waits=[(c_v, neglse_v), bank_free[bk]], incs=[(c_t, 1)])
            bank_gen[bk] = c_t.n
            neglse_guard[0] = (c_t, c_t.n)
            pm.do("v", (lambda bk=bk: lambda e:
                        e.tensor_copy(rows_cd[:, 0:128], B[bk][0:32, 0:128]))(),
                  waits=[(c_t, c_t.n), d_lst], incs=[(c_v, 1)])
            pm.do("v", (lambda bk=bk: lambda e:
                        e.tensor_tensor(rows_cd[:, 128:256], B[bk][0:32, 0:128],
                                        rows_cd[:, 0:128], ALU.subtract))(),
                  waits=[c_v], incs=[(c_v, 1)])
            rows_v = c_v.n
            bank_free[bk] = (c_v, rows_v)
            scr = lse_scr[cc % 2]
            pm.do("g", (lambda scr=scr: lambda e:
                        e.dma_start(out=scr, in_=rows_cd[:]))(),
                  waits=[(c_v, rows_v), lse_guard[cc % 2]], incs=[(d_lst, 16)])
            fi, fb = flat_bufs[cc % 2]
            pm.do("g", (lambda scr=scr, fi=fi: lambda e: e.dma_start(
                        out=fi.rearrange("two (h t) -> two h t", h=16),
                        in_=scr[0:16, :].rearrange("h (two t) -> two h t",
                                                   two=2)))(),
                  waits=[d_lst, flat_guard[cc % 2], hflat_guard, pq_guard],
                  incs=[(d_lsl, 16)])
            scr5 = scr[16:32, :].rearrange(
                "(half q2 q4) (two t) -> q2 two half q4 t", half=2, q2=2, two=2)
            for q2i in range(2):
                for twoi in range(2):
                    r = 2 * twoi + q2i
                    pm.do("g", (lambda fb=fb, r=r, scr5=scr5, q2i=q2i,
                                twoi=twoi: lambda e: e.dma_start(
                                    out=fb[r:r + 1, :].rearrange(
                                        "o (half q4 t) -> o half q4 t",
                                        half=2, q4=4),
                                    in_=scr5[q2i, twoi]))(),
                          incs=[(d_lsl, 16)])
            return d_lsl.n

        def emit_p2_av(cc, flat_v, part):
            c = 4 * sc + cc
            ncj = c + 1
            fi, fb = flat_bufs[cc % 2]
            o2 = ones2_sl[cc % 2]
            s4 = sel4_sl[cc % 2]
            exp_done = exp_done_st[0]
            avd = (c_t, av_done_prev[0])
            gens = ((("i", 0), ("i", 1), ("n", 0)) if part == 0 else
                    (("i", 2), ("i", 3), ("n", 1)))
            if part == 2:
                gens = ()
            for kind, g in gens:
                bk = pool_next()
                if kind == "i":
                    for qi, h in enumerate(grp_heads[g]):
                        base = (h % 2) * 64
                        fc = h // 2
                        pm.do("t", (lambda bk=bk, qi=qi, base=base, fc=fc,
                                    cc=cc: lambda e: e.matmul(
                                        B[bk][:, qi * 128:(qi + 1) * 128],
                                        k_fm[base:base + 64, fc * 512 + cc * 128:
                                             fc * 512 + cc * 128 + 128],
                                        q_fm[base:base + 64, fc * 512 + cc * 128:
                                             fc * 512 + cc * 128 + 128],
                                        start=(qi == 0), stop=False,
                                        skip_group_check=True))(),
                              waits=([bank_free[bk]] if qi == 0 else ()))
                    pm.do("t", (lambda bk=bk, g=g, fi=fi, o2=o2: lambda e:
                                e.matmul(B[bk][:], o2,
                                         fi[:, g * 512:(g + 1) * 512],
                                         start=False, stop=True,
                                         skip_group_check=True))(),
                          waits=[(d_lsl, flat_v)], incs=[(c_t, 1)])
                else:
                    par8 = g
                    pm.do("t", (lambda bk=bk, g=g, fb=fb, s4=s4: lambda e:
                                e.matmul(B[bk][:], s4,
                                         fb[:, g * 512:(g + 1) * 512],
                                         start=True, stop=False,
                                         skip_group_check=True))(),
                          waits=[bank_free[bk], (d_lsl, flat_v)])
                    hs = [h for h in range(16) if h % 2 == par8]
                    for j, h in enumerate(hs):
                        base = (h % 2) * 64
                        fc = h // 2
                        q8 = cix(h) % 8
                        sl = (q8 % 4) * 128
                        pb = 64 * (q8 // 4)
                        pm.do("t", (lambda bk=bk, sl=sl, pb=pb, base=base,
                                    fc=fc, cc=cc, ncj=ncj, j=j: lambda e: e.matmul(
                                        B[bk][pb:pb + ncj, sl:sl + 128],
                                        km_fm[base:base + 64,
                                              fc * 64:fc * 64 + ncj],
                                        q_fm[base:base + 64, fc * 512 + cc * 128:
                                             fc * 512 + cc * 128 + 128],
                                        start=False, stop=(j == 7),
                                        skip_group_check=True))(),
                              incs=([(c_t, 1)] if j == 7 else ()))
                if kind == "i":
                    pm.do("s", (lambda bk=bk, g=g: lambda e:
                                e.activation(A1T[:, g * 512:(g + 1) * 512],
                                             B[bk][:], AF.Exp))(),
                          waits=[(c_t, c_t.n), avd] + vm_latest[0],
                          incs=[(c_s, 1)])
                else:
                    pm.do("s", (lambda bk=bk, g=g: lambda e:
                                e.activation(A2T[:, g * 512:(g + 1) * 512],
                                             B[bk][:], AF.Exp))(),
                          waits=[(c_t, c_t.n), avd] + vm_latest[0],
                          incs=[(c_s, 1)])
                exp_done[(kind, g)] = c_s.n
                bank_free[bk] = (c_s, c_s.n)

            if part != 2:
                return
            first_in_bank = {}
            av_first = True
            for h in [hh for hh in range(16) if hh % 2 == 0] + \
                     [hh for hh in range(16) if hh % 2 == 1]:
                ci = cix(h)
                ec = h // 2
                abk = 6 + ec // 4
                qq = ec % 4
                pbase = (h % 2) * 64
                w = []
                if av_first:
                    w = [bank_free[6], bank_free[7]] + vm_ready
                    av_first = False
                if ci % 4 == 0:
                    w = w + [(c_s, exp_done[("i", ci // 4)])]
                stf = not first_in_bank.get((abk, pbase), False)
                pm.do("t", (lambda abk=abk, qq=qq, pbase=pbase, h=h, cc=cc,
                            ci=ci, stf=stf: lambda e: e.matmul(
                                B[abk][pbase:pbase + 64, qq * 128:(qq + 1) * 128],
                                v_tm[:, cc * 1024 + h * 64:cc * 1024 + h * 64 + 64],
                                A1T[:, ci * 128:(ci + 1) * 128],
                                start=stf, stop=False,
                                skip_group_check=True))(),
                      waits=w)
                first_in_bank[(abk, pbase)] = True
            for h in [hh for hh in range(16) if hh % 2 == 0] + \
                     [hh for hh in range(16) if hh % 2 == 1]:
                ci = cix(h)
                ec = h // 2
                abk = 6 + ec // 4
                qq = ec % 4
                pbase = (h % 2) * 64
                q8 = ci % 8
                pb2 = 64 * (q8 // 4)
                sl = (ci // 8) * 512 + (q8 % 4) * 128
                wi = []
                if h == 0:
                    wi = [(c_s, exp_done[("n", 0)])]
                elif h == 1:
                    wi = [(c_s, exp_done[("n", 1)])]
                pm.do("t", (lambda abk=abk, qq=qq, pbase=pbase, h=h, ncj=ncj,
                            sl=sl, pb2=pb2, cc=cc: lambda e: e.matmul(
                                B[abk][pbase:pbase + 64, qq * 128:(qq + 1) * 128],
                                vm_cm[pb2:pb2 + ncj, h * 64:h * 64 + 64],
                                A2T[pb2:pb2 + ncj, sl:sl + 128],
                                start=False, stop=False,
                                skip_group_check=True))(),
                      waits=wi,
                      incs=([(c_t, 1)] if h == 15 else ()))
            av_done_prev[0] = c_t.n
            flat_guard[cc % 2] = (c_t, c_t.n)
            lse_guard[cc % 2] = (c_t, c_t.n)
            for abk in (6, 7):
                pm.do("v", (lambda abk=abk, cc=cc: lambda e: e.tensor_copy(
                            attn_fm[:, 0:4096].rearrange("p (ec f) -> p ec f",
                                                         ec=8)[
                                :, (abk - 6) * 4:(abk - 6) * 4 + 4,
                                cc * 128:cc * 128 + 128],
                            B[abk][:].rearrange("p (a b) -> p a b", a=4)))(),
                      waits=[(c_t, av_done_prev[0]), attn_guard],
                      incs=[(c_v, 1)])
                bank_free[abk] = (c_v, c_v.n)

        for cc in range(4):
            emit_vsum(cc)
        vm_ready_cc = {cc: vm_latest[0] for cc in range(4)}
        nl0 = emit_p1(0)
        fl0 = emit_lse_bounce(0, nl0)
        flat_v = {0: fl0}
        for rc in range(2):
            bk = pool_next()
            w = [(c_v, xn_ready), bank_free[bk]]
            for dc in range(8):
                pm.do("t", (lambda bk=bk, rc=rc, dc=dc: lambda e: e.matmul(
                            B[bk][:],
                            U1[:, dc * 256 + rc * 128:dc * 256 + rc * 128 + 128],
                            xn_fm[:, dc * 512:(dc + 1) * 512],
                            start=(dc == 0), stop=(dc == 7)))(),
                      waits=(w if dc == 0 else ()),
                      incs=([(c_t, 1)] if dc == 7 else ()))
            bank_gen[bk] = c_t.n
            pm.do("v", (lambda bk=bk, rc=rc: lambda e:
                        e.tensor_copy(P_fm[:, rc * 512:(rc + 1) * 512], B[bk][:]))(),
                  waits=[(c_t, bank_gen[bk]), pq_guard], incs=[(c_v, 1)])
            bank_free[bk] = (c_v, c_v.n)
        p_done_sc = c_v.n
        xnfm_guard = (c_t, c_t.n)
        for cc in range(4):
            if cc + 1 < 4:
                nl = emit_p1(cc + 1)
            vm_ready = vm_ready_cc[cc]
            exp_done_st[0] = {}
            emit_p2_av(cc, flat_v[cc], 0)
            if cc + 1 < 4:
                flat_v[cc + 1] = emit_lse_bounce(cc + 1, nl)
            emit_p2_av(cc, flat_v[cc], 1)
            emit_p2_av(cc, flat_v[cc], 2)
        a_guard = (c_t, av_done_prev[0])
        attn_done_v = c_v.n
        kq_guard = (c_t, c_t.n)

        if sc + 1 < nsc:
            ln_next = emit_ln(sc + 1, ln_sums_next)

        p_done = p_done_sc

        h_gelu = [0] * 32
        qw_first = True
        for g in range(34):
            if g < 32:
                bk = 2 + g % 4
                w = [(c_v, p_done), bank_free[bk]]
                for rc in range(2):
                    pm.do("t", (lambda bk=bk, g=g, rc=rc: lambda e: e.matmul(
                                B[bk][:],
                                V1[:, rc * 4096 + g * 128:rc * 4096 + g * 128 + 128],
                                P_fm[:, rc * 512:(rc + 1) * 512],
                                start=(rc == 0), stop=(rc == 1)))(),
                          waits=(w if rc == 0 else ()),
                          incs=([(c_t, 1)] if rc == 1 else ()))
                bank_gen[bk] = c_t.n
                pm.do("s", (lambda bk=bk, g=g: lambda e: e.activation(
                            H_rot[:, (g % 4) * 512:(g % 4) * 512 + 512],
                            B[bk][:], AF.Gelu_apprx_tanh,
                            bias=b1_f[:, g:g + 1]))(),
                      waits=[(c_t, bank_gen[bk]), hrot_guard[g % 4],
                             vrow_guard[0], vrow_guard[1],
                             flat_guard[0], flat_guard[1]],
                      incs=[(c_s, 1)])
                h_gelu[g] = c_s.n
                bank_free[bk] = (c_s, c_s.n)
            gq = g - 2
            if 0 <= gq < 32:
                w = [(c_s, h_gelu[gq])]
                if qw_first:
                    w += [bank_free[6], bank_free[7], (c_v, attn_done_v)]
                for rc in range(2):
                    pm.do("t", (lambda gq=gq, rc=rc: lambda e: e.matmul(
                                B[6 + rc][:],
                                U2[:, gq * 256 + rc * 128:gq * 256 + rc * 128 + 128],
                                H_rot[:, (gq % 4) * 512:(gq % 4) * 512 + 512],
                                start=(gq == 0), stop=(gq == 31),
                                skip_group_check=True))(),
                          waits=(w if rc == 0 else ()),
                          incs=([(c_t, 1)] if (gq == 31 and rc == 1) else ()))
                if qw_first:
                    qw_first = False
                hrot_guard[gq % 4] = (c_t, c_t.n)
        q_mm_done = c_t.n
        hflat_guard = (c_t, q_mm_done)
        if sc + 1 < nsc:
            xn_ready_next = emit_xnt(ln_next)
        bank_gen[6] = bank_gen[7] = c_t.n
        for rc in range(2):
            pm.do("v", (lambda rc=rc: lambda e:
                        e.tensor_copy(Q_fm[:, rc * 512:(rc + 1) * 512],
                                      B[6 + rc][:]))(),
                  waits=[(c_t, q_mm_done)], incs=[(c_v, 1)])
            bank_free[6 + rc] = (c_v, c_v.n)
        pq_guard = (c_t, 0)

        for st in range(4):
            bks = [st % 2 * 2, st % 2 * 2 + 1]
            slot = st % 2
            for half in range(2):
                bk = bks[half]
                w = [(c_v, attn_done_v), (c_v, c_v.n), bank_free[bk]]
                for ec in range(8):
                    pm.do("t", (lambda bk=bk, st=st, half=half, ec=ec: lambda e:
                                e.matmul(
                                    B[bk][:],
                                    attn_fm[:, ec * 512 + st * 128:
                                            ec * 512 + st * 128 + 128],
                                    woT[:, ec * 1024 + half * 512:
                                        ec * 1024 + half * 512 + 512],
                                    start=(ec == 0), stop=False))(),
                          waits=(w if ec == 0 else ()))
                for rc in range(2):
                    pm.do("t", (lambda bk=bk, st=st, half=half, rc=rc: lambda e:
                                e.matmul(
                                    B[bk][:],
                                    Q_fm[:, rc * 512 + st * 128:
                                         rc * 512 + st * 128 + 128],
                                    V2[:, rc * 1024 + half * 512:
                                       rc * 1024 + half * 512 + 512],
                                    start=False, stop=False))())
                pm.do("t", (lambda bk=bk, half=half: lambda e: e.matmul(
                            B[bk][:], ones2[0:1, :],
                            bob2_row[0:1, half * 512:half * 512 + 512],
                            start=False, stop=True))(),
                      incs=[(c_t, 1)])
                bank_gen[bk] = c_t.n
                pm.do("v", (lambda bk=bk, st=st, half=half, slot=slot, xb=xb:
                            lambda e: e.tensor_tensor(
                                out_st[:, slot * 1024 + half * 512:
                                       slot * 1024 + half * 512 + 512],
                                B[bk][:],
                                xb[:, st * 1024 + half * 512:
                                   st * 1024 + half * 512 + 512],
                                ALU.add))(),
                      waits=[(c_t, bank_gen[bk]), stage_guard[slot]],
                      incs=[(c_v, 1)])
                bank_free[bk] = (c_v, c_v.n)
            ev = c_v.n
            pm.do("y", (lambda sc=sc, st=st, slot=slot: lambda e: e.dma_start(
                        out=out_h[sc * 512 + st * 128:sc * 512 + st * 128 + 128, :],
                        in_=out_st[:, slot * 1024:slot * 1024 + 1024]))(),
                  waits=[(c_v, ev)], incs=[(d_op[slot], 16)])
            stage_guard[slot] = (d_op[slot], d_op[slot].n)
        x_guard[par] = (c_v, c_v.n)
        pq_guard = (c_t, c_t.n)
        attn_guard = (c_v, c_v.n)


    if debug:
        d_dbg = Ctr(nc, "d_dbg")
        dbg_list = [("dbg_xnfm", xn_fm), ("dbg_q", q_fm), ("dbg_k", k_fm),
                    ("dbg_v", v_tm), ("dbg_km", km_fm), ("dbg_vm", vm_cm),
                    ("dbg_attn", attn_fm), ("dbg_P", P_fm), ("dbg_Q", Q_fm),
                    ("dbg_negm1", negm1), ("dbg_negm2", negm2),
                    ("dbg_z1", z1), ("dbg_z2", z2), ("dbg_z2b", z2b), ("dbg_neglse", neglse),
                    ("dbg_A1T", A1T), ("dbg_A2T", A2T), ("dbg_rowscd", rows_cd),
                    ("dbg_bqeff", bq_eff), ("dbg_bkf", bk_f), ("dbg_b1f", b1_f),
                    ("dbg_bob2", bob2_row), ("dbg_bvrow", bv_row),
                    ("dbg_wqT", wqT), ("dbg_U1", U1)]
        allw = [c_t, c_v, c_s]
        for (nm, tt) in dbg_list:
            od = ap(nc.dram_tensor(nm, list(tt.shape), tt.dtype,
                                   kind="ExternalOutput"))
            pm.do("g", (lambda od=od, tt=tt: lambda e:
                        e.dma_start(out=od, in_=tt[:]))(),
                  waits=[c_t, c_v, c_s], incs=[(d_dbg, 16)])

    fin_w = [d_op[0], d_op[1], d_lst, d_lsl, d_vm[0], d_vm[1]]
    if debug:
        pm.do("g", lambda e: e.nop(), waits=[d_dbg])
    pm.do("y", lambda e: e.nop(), waits=fin_w)

    with nc.Block("main") as blk2:
        pm.emit(blk2)

    return nc


_NC_CACHE = {}


def kernel(**inputs):
    nsc = NSC_FULL
    if nsc not in _NC_CACHE:
        _NC_CACHE[nsc] = build_nc(nsc)
    nc = _NC_CACHE[nsc]
    x = np.ascontiguousarray(inputs["L0_attn_hidden_states"])
    weights = {k: np.ascontiguousarray(v) for k, v in inputs.items()
               if k != "L0_attn_hidden_states"}
    in_maps = []
    for core in range(NCORE):
        m = dict(weights)
        m["L0_attn_hidden_states"] = x[core]
        in_maps.append(m)
    res = run_bass_kernel_spmd(nc, in_maps, core_ids=list(range(NCORE)))
    return np.stack([res.results[i]["out"] for i in range(NCORE)], axis=0)


# version 15
# speedup vs baseline: 1.2092x; 1.0165x over previous
import numpy as np
import ml_dtypes
import concourse.bass as bass
import concourse.mybir as mybir
from concourse.bass_utils import run_bass_kernel_spmd

F32 = mybir.dt.float32
BF16 = mybir.dt.bfloat16
AF = mybir.ActivationFunctionType
ALU = mybir.AluOpType
AX = mybir.AxisListType
BF = ml_dtypes.bfloat16

NCORE = 8
S = 8192
E = 1024
NH = 16
D = 64
T = 128
NSC_FULL = 16
SCT = 512
DFF = 4096
RK = 256


class Ctr:

    def __init__(self, nc, name):
        self.h = nc.alloc_semaphore(name)
        self.n = 0

    def snap(self):
        return (self, self.n)


class Prog:

    ENGINES = ("t", "v", "s", "y", "g")

    def __init__(self):
        self.steps = []

    def do(self, eng, fn, waits=(), incs=()):
        w = [x if isinstance(x, tuple) else x.snap() for x in waits]
        self.steps.append((eng, w, fn, list(incs)))
        for (c, k) in incs:
            c.n += k

    def emit(self, block):
        def mk(eng_key):
            def run(engine):
                done = {}
                for (ek, waits, fn, incs) in self.steps:
                    if ek != eng_key:
                        continue
                    for (c, val) in waits:
                        if val > 0 and done.get(id(c), -1) < val:
                            engine.wait_ge(c.h, val)
                            done[id(c)] = val
                    instr = fn(engine)
                    for (c, k) in incs:
                        instr.then_inc(c.h, k)
            return run

        block.tensor(mk("t"))
        block.vector(mk("v"))
        block.scalar(mk("s"))
        block.sync(mk("y"))
        block.gpsimd(mk("g"))


def build_nc(nsc=NSC_FULL, debug=False):
    nc = bass.Bass("TRN2", target_bir_lowering=False)
    ap = lambda t: t.ap()

    x_h = ap(nc.dram_tensor("L0_attn_hidden_states", [S, E], F32, kind="ExternalInput"))
    wq_h = ap(nc.dram_tensor("L0_attn_wq", [E, E], F32, kind="ExternalInput"))
    bq_h = ap(nc.dram_tensor("L0_attn_bq", [E], F32, kind="ExternalInput"))
    wk_h = ap(nc.dram_tensor("L0_attn_wk", [E, E], F32, kind="ExternalInput"))
    bk_h = ap(nc.dram_tensor("L0_attn_bk", [E], F32, kind="ExternalInput"))
    wv_h = ap(nc.dram_tensor("L0_attn_wv", [E, E], F32, kind="ExternalInput"))
    bv_h = ap(nc.dram_tensor("L0_attn_bv", [E], F32, kind="ExternalInput"))
    wo_h = ap(nc.dram_tensor("L0_attn_wo", [E, E], F32, kind="ExternalInput"))
    bo_h = ap(nc.dram_tensor("L0_attn_bo", [E], F32, kind="ExternalInput"))
    u1_h = ap(nc.dram_tensor("L0_ffn_U1", [E, RK], F32, kind="ExternalInput"))
    v1_h = ap(nc.dram_tensor("L0_ffn_V1", [RK, DFF], F32, kind="ExternalInput"))
    b1_h = ap(nc.dram_tensor("L0_ffn_b1", [DFF], F32, kind="ExternalInput"))
    u2_h = ap(nc.dram_tensor("L0_ffn_U2", [DFF, RK], F32, kind="ExternalInput"))
    v2_h = ap(nc.dram_tensor("L0_ffn_V2", [RK, E], F32, kind="ExternalInput"))
    b2_h = ap(nc.dram_tensor("L0_ffn_b2", [E], F32, kind="ExternalInput"))
    out_h = ap(nc.dram_tensor("out", [S, E], F32, kind="ExternalOutput"))
    lse_scr = [ap(nc.dram_tensor(f"lse_scr{i}", [32, 256], BF16, kind="Internal"))
               for i in range(2)]

    ident_c = nc.inline_tensor(np.eye(128, dtype=np.float32), name="c_ident")
    identb_c = nc.inline_tensor(np.eye(128, dtype=BF), name="c_identb")
    o2d = np.zeros((34, 128), dtype=BF)
    o2d[0, :] = 1
    o2d[1, :] = 1
    o2d[32, :] = 1
    o2d[33, :] = 1
    ones2_c = nc.inline_tensor(o2d, name="c_ones2")
    sel4 = np.zeros((68, 128), dtype=BF)
    for b0 in (0, 64):
        sel4[b0 + 0, 0:64] = 1
        sel4[b0 + 1, 64:128] = 1
        sel4[b0 + 2, 0:64] = 1
        sel4[b0 + 3, 64:128] = 1
    sel4_c = nc.inline_tensor(sel4, name="c_sel4")
    onescol_c = nc.inline_tensor(np.ones((128, 1), dtype=BF), name="c_onescol")
    cv = np.zeros((128, 8), dtype=np.float32)
    cv[:, 0] = 1e-5
    cv[:, 1] = -0.5
    cv[:, 2] = 0.125
    cv[:, 3] = 1.0 / 128
    cv[:, 4] = 128.0
    cv[:, 5] = -20.0
    cv[:, 6] = 1.0 / 1024
    cvals_c = nc.inline_tensor(cv, name="c_cvals")

    sb = nc.alloc_sbuf_tensor
    wqT = sb("wqT", [128, 8192], BF16)
    wkT = sb("wkT", [128, 8192], BF16)
    wvT = sb("wvT", [128, 8192], BF16)
    woT = sb("woT", [128, 8192], BF16)
    U1 = sb("U1", [128, 2048], BF16)
    V1 = sb("V1", [128, 8192], BF16)
    U2 = sb("U2", [128, 8192], BF16)
    V2 = sb("V2", [128, 2048], BF16)
    x_buf = sb("x_buf", [128, 8192], F32)
    out_st = sb("out_st", [128, 2048], F32)
    xn_fm = sb("xn_fm", [128, 4096], BF16)
    q_fm = sb("q_fm", [128, 4096], BF16)
    k_fm = sb("k_fm", [128, 4096], BF16)
    v_tm = sb("v_tm", [128, 4096], BF16)
    attn_fm = sb("attn_fm", [128, 4096], BF16)
    xn_tm = v_tm
    A1T = sb("A1T", [128, 2048], BF16)
    A2T = sb("A2T", [128, 1024], BF16)
    H_rot = sb("H_rot", [128, 2048], BF16)
    P_fm = sb("P_fm", [128, 1024], BF16)
    Q_fm = sb("Q_fm", [128, 1024], BF16)
    km_fm = sb("km_fm", [128, 512], BF16)
    vm_cm = sb("vm_cm", [128, 1024], BF16)
    ident = sb("identity", [128, 128], F32)
    ident_b = sb("identity_b", [128, 128], BF16)
    ones2 = sb("ones2", [34, 128], BF16)
    sel4_t = sb("sel4", [68, 128], BF16)
    ones_col = sb("ones_col", [128, 1], BF16)
    cvals = sb("cvals", [128, 8], F32)
    bq_eff = sb("bq_eff", [128, 8], F32)
    bk_f = sb("bk_f", [128, 8], F32)
    b1_f = sb("b1_f", [128, 32], F32)
    bv_row = sb("bv_row", [1, 1024], BF16)
    bob2_row = sb("bob2_row", [1, 1024], BF16)
    stats = sb("stats", [128, 16], F32)
    negm1 = sb("negm1", [128, 16], F32)
    negm2 = sb("negm2", [128, 16], F32)
    z1 = sb("z1", [128, 16], F32)
    z2 = sb("z2", [128, 16], F32)
    z2b = negm1
    lnz = sb("lnz", [128, 32], F32)
    neglse = sb("neglse", [128, 32], F32)
    rows_cd = sb("rows_cd", [32, 256], BF16)
    kcol = sb("kcol", [128, 1], F32)
    bias_r = out_st[0:32, 0:128]
    flat_bufs = {0: (H_rot[0:2, 0:2048], H_rot[64:68, 0:1024]),
                 1: (H_rot[32:34, 0:2048], Q_fm[0:4, 0:1024])}
    vrow_slots = None

    c_t = Ctr(nc, "c_t")
    c_v = Ctr(nc, "c_v")
    c_s = Ctr(nc, "c_s")
    d_xp = [Ctr(nc, "d_x0"), Ctr(nc, "d_x1")]
    d_cst = Ctr(nc, "d_cst")
    d_ffw = Ctr(nc, "d_ffw")
    d_st = [Ctr(nc, "d_st0"), Ctr(nc, "d_st1")]
    d_lst = Ctr(nc, "d_lst")
    d_lsl = Ctr(nc, "d_lsl")
    d_vm = [Ctr(nc, "d_vm0"), Ctr(nc, "d_vm1")]
    d_op = [Ctr(nc, "d_o0"), Ctr(nc, "d_o1")]

    ps = Prog()
    if debug:
        mlist = (km_fm, vm_cm, A1T, A2T, H_rot, P_fm, Q_fm, attn_fm,
                 q_fm, k_fm, v_tm, xn_fm, negm1, negm2, z1, z2, lnz,
                 neglse, rows_cd, stats, out_st, x_buf, kcol)
        for i, tt in enumerate(mlist):
            ps.do("v", (lambda tt=tt: lambda e: e.memset(tt[:], 0.0))(),
                  incs=([(c_v, 1)] if i == len(mlist) - 1 else ()))

    def dma(eng_attr, out, in_):
        return lambda e: e.dma_start(out=out, in_=in_)

    for (dst, src) in ((ident, ident_c), (ident_b, identb_c), (ones2, ones2_c),
                       (sel4_t, sel4_c), (ones_col, onescol_c), (cvals, cvals_c)):
        ps.do("y", dma("y", dst[:], src.ap()), incs=[(d_cst, 16)])
    xr = x_h.rearrange("(sc st p) f -> sc p st f", p=128, st=4)
    ps.do("y", dma("y", x_buf[:, 0:4096].rearrange("p (st f) -> p st f", st=4),
                   xr[0]), waits=[c_v], incs=[(d_xp[0], 16)])

    ps.do("g", dma("g", U1[:].rearrange("p (dc r) -> p dc r", dc=8),
                   u1_h.rearrange("(dc p) r -> p dc r", p=128)),
          waits=[c_v], incs=[(d_ffw, 16)])
    ps.do("g", dma("g", V1[:].rearrange("p (rc f) -> p rc f", rc=2),
                   v1_h.rearrange("(rc p) f -> p rc f", p=128)), incs=[(d_ffw, 16)])
    ps.do("g", dma("g", U2[:].rearrange("p (gc r) -> p gc r", gc=32),
                   u2_h.rearrange("(gc p) r -> p gc r", p=128)), incs=[(d_ffw, 16)])
    ps.do("g", dma("g", V2[:].rearrange("p (rc d) -> p rc d", rc=2),
                   v2_h.rearrange("(rc p) d -> p rc d", p=128)), incs=[(d_ffw, 16)])
    ps.do("g", dma("g", bias_r[0:8, :], bq_h.rearrange("(a t) -> a t", t=128)),
          incs=[(d_ffw, 16)])

    ps.do("g", dma("g", bv_row[:], bv_h.rearrange("(a t) -> a t", t=1024)),
          incs=[(d_ffw, 16)])
    ps.do("g", dma("g", out_st[64:65, 0:1024], bo_h.rearrange("(a t) -> a t", t=1024)),
          incs=[(d_ffw, 16)])
    ps.do("g", dma("g", out_st[64:65, 1024:2048], b2_h.rearrange("(a t) -> a t", t=1024)),
          incs=[(d_ffw, 16)])

    ps.do("v", lambda e: e.tensor_tensor(bob2_row[:], out_st[64:65, 0:1024],
                                         out_st[64:65, 1024:2048], ALU.add),
          waits=[d_ffw], incs=[(c_v, 1)])

    pb = nc.alloc_psum_tensor
    B = [pb(f"B{i}", [128, 512], F32) for i in range(8)]

    for bi in range(8):
        ps.do("v", (lambda bi=bi: lambda e: e.memset(B[bi][:], 0.0))(),
              incs=([(c_v, 1)] if bi == 7 else ()))
    dbg_v0 = c_v.n
    wlist = [(wq_h, wqT, 2), (wk_h, wkT, None), (wv_h, wvT, None),
             (wo_h, woT, None)]
    stage = H_rot
    ev_after = {}
    for wi, (wsrc, dstT, scale_col) in enumerate(wlist):
        for fc in range(8):
            slot = (wi * 8 + fc) % 2
            stg = stage[:, slot * 1024:(slot + 1) * 1024]
            w_st = [(c_t, ev_after.get(slot, 0)), (c_v, dbg_v0)]
            ps.do("g", dma("g", stg, wsrc[fc * 128:(fc + 1) * 128, :]),
                  waits=w_st, incs=[(d_st[slot], 16)])
            wts = [(d_st[slot], d_st[slot].n), (d_cst, d_cst.n),
                   (c_v, ev_after.get(("e", slot), dbg_v0))]
            for half in range(2):
                bank = B[slot * 2 + half]
                for eci in range(4):
                    ec = half * 4 + eci
                    ps.do("t", (lambda b=bank, sg=stg, ec=ec, eci=eci: lambda e:
                                e.matmul(b[:, eci * 128:(eci + 1) * 128],
                                         sg[:, ec * 128:(ec + 1) * 128],
                                         ident_b[:], start=(eci == 0),
                                         stop=(eci == 3)))(),
                          waits=(wts if (half == 0 and eci == 0) else ()),
                          incs=([(c_t, 1)] if (half == 1 and eci == 3) else ()))
            ev_after[slot] = c_t.n
            def mkev(slot=slot, dstT=dstT, fc=fc, scale_col=scale_col):
                def f(e):
                    o = dstT[:, 0:8192].rearrange("p (ec f) -> p ec f", ec=8)[
                        :, :, fc * 128:(fc + 1) * 128]
                    i0 = B[slot * 2][:].rearrange("p (a b) -> p a b", a=4)
                    i1 = B[slot * 2 + 1][:].rearrange("p (a b) -> p a b", a=4)
                    if scale_col is not None:
                        e.tensor_scalar(o[:, 0:4], i0,
                                        cvals[:, scale_col:scale_col + 1],
                                        None, ALU.mult)
                        return e.tensor_scalar(o[:, 4:8], i1,
                                               cvals[:, scale_col:scale_col + 1],
                                               None, ALU.mult)
                    e.tensor_copy(o[:, 0:4], i0)
                    return e.tensor_copy(o[:, 4:8], i1)
                return f
            ps.do("v", mkev(), waits=[c_t], incs=[(c_v, 1)])
            ev_after[("e", slot)] = c_v.n

    with nc.Block("setup") as blk1:
        ps.emit(blk1)

    setup_t, setup_v, setup_s = c_t.n, c_v.n, c_s.n

    pm = Prog()

    pm.do("t", lambda e: e.transpose(B[0][0:128, 0:8], bias_r[0:8, :], ident[0:8, 0:8]),
          waits=[(c_v, setup_v)], incs=[(c_t, 1)])
    t_bias0 = c_t.n
    pm.do("g", dma("g", bias_r[0:8, :], bk_h.rearrange("(a t) -> a t", t=128)),
          waits=[(c_t, t_bias0)], incs=[(d_ffw, 16)])
    pm.do("t", lambda e: e.transpose(B[0][0:128, 8:16], bias_r[0:8, :], ident[0:8, 0:8]),
          waits=[(d_ffw, d_ffw.n)], incs=[(c_t, 1)])
    t_bias1 = c_t.n
    pm.do("v", lambda e: e.tensor_scalar(bq_eff[:], B[0][:, 0:8],
                                         cvals[:, 2:3], None, ALU.mult),
          waits=[(c_t, t_bias1)], incs=[(c_v, 1)])
    pm.do("v", lambda e: e.tensor_copy(bk_f[:], B[0][:, 8:16]), incs=[(c_v, 1)])
    v_bias1 = c_v.n
    pm.do("g", dma("g", bias_r[:], b1_h.rearrange("(a t) -> a t", t=128)),
          waits=[(c_v, v_bias1)], incs=[(d_ffw, 16)])
    pm.do("t", lambda e: e.transpose(B[0][0:128, 16:48], bias_r[:], ident[0:32, 0:32]),
          waits=[(d_ffw, d_ffw.n), (c_v, v_bias1)], incs=[(c_t, 1)])
    pm.do("v", lambda e: e.tensor_copy(b1_f[:], B[0][:, 16:48]),
          waits=[c_t], incs=[(c_v, 1)])

    bank_gen = [c_t.snap()[1]] * 8
    bank_free = [(c_v, 0)] * 8

    def after_all(*cs):
        return [c.snap() for c in cs]

    x_guard = {0: (c_v, 0), 1: (c_v, 0)}
    lse_guard = {0: (c_t, 0), 1: (c_t, 0)}
    flat_guard = {0: (c_t, 0), 1: (c_t, 0)}
    vrow_guard = {0: (d_vm[0], 0), 1: (d_vm[1], 0)}
    vm_ready = [(d_vm[0], 0), (d_vm[1], 0)]
    a_guard = (c_t, 0)
    stage_guard = {0: (d_op[0], 0), 1: (d_op[1], 0)}
    kq_guard = (c_t, 0)
    xnfm_guard = (c_t, 0)
    ones2_sl = {0: ones2[0:2, :], 1: ones2[32:34, :]}
    sel4_sl = {0: sel4_t[64:68, :], 1: sel4_t[0:4, :]}
    vrow_slots = {0: A1T[0:1, 0:1024], 1: A2T[0:1, 0:1024]}
    av_done_prev = [0]
    exp_done_st = [{}]
    neglse_guard = [(c_t, 0)]
    attn_guard = (c_v, 0)
    hrot_guard = [(c_t, 0)] * 4
    hflat_guard = (c_t, 0)
    pq_guard = (c_t, 0)

    def emit_ln_sums(scx):
        parx = scx % 2
        xbx = x_buf[:, parx * 4096:(parx + 1) * 4096]
        for st in range(4):
            xt = xbx[:, st * 1024:(st + 1) * 1024]
            s_sum = stats[:, st * 4:st * 4 + 1]
            s_nm = stats[:, st * 4 + 2:st * 4 + 3]
            pm.do("v", (lambda xt=xt, s_sum=s_sum: lambda e:
                        e.reduce_sum(s_sum, xt, axis=AX.X))(),
                  waits=[d_xp[parx], x_guard[parx], c_v.snap()], incs=[(c_v, 1)])
            pm.do("v", (lambda s_sum=s_sum, s_nm=s_nm: lambda e:
                        e.tensor_scalar(s_nm, s_sum, -1.0 / 1024, None,
                                        ALU.mult))(),
                  waits=[c_v], incs=[(c_v, 1)])
        return c_v.n

    def emit_ln_tile(scx, st, avg):
        parx = scx % 2
        xbx = x_buf[:, parx * 4096:(parx + 1) * 4096]
        xt = xbx[:, st * 1024:(st + 1) * 1024]
        xnt = xn_tm[:, st * 1024:(st + 1) * 1024]
        s_sq = stats[:, st * 4 + 1:st * 4 + 2]
        s_nm = stats[:, st * 4 + 2:st * 4 + 3]
        s_rs = stats[:, st * 4 + 3:st * 4 + 4]
        pm.do("s", (lambda xt=xt, xnt=xnt, s_sq=s_sq, s_nm=s_nm: lambda e:
                    e.activation(xnt, xt, AF.Square, bias=s_nm,
                                 accum_out=s_sq))(),
              waits=[ln_sums_g[0], avg, c_s.snap()], incs=[(c_s, 1)])
        pm.do("s", (lambda s_sq=s_sq, s_rs=s_rs: lambda e:
                    e.activation(s_rs, s_sq, AF.Ln, bias=cvals[:, 0:1],
                                 scale=cvals[:, 6:7]))(),
              waits=[c_s], incs=[(c_s, 1)])
        pm.do("s", (lambda s_rs=s_rs: lambda e:
                    e.activation(s_rs, s_rs, AF.Exp, scale=cvals[:, 1:2]))(),
              waits=[c_s], incs=[(c_s, 1)])
        sw2 = [c_s.snap()]
        pm.do("v", (lambda xt=xt, xnt=xnt, s_nm=s_nm, s_rs=s_rs: lambda e:
                    e.tensor_scalar(xnt, xt, s_nm, s_rs, ALU.add,
                                    ALU.mult))(),
              waits=sw2 + [c_v.snap()], incs=[(c_v, 1)])
        return c_v.n



    ln_sums_g = [(c_v, 0)]
    cix = lambda h: (h % 2) * 8 + h // 2
    grp_heads = [[h for h in range(16) if cix(h) // 4 == g] for g in range(4)]
    pool_i = [0]

    def pool_next():
        bk = pool_i[0]
        pool_i[0] = (bk + 1) % 6
        return bk

    def emit_xnt(ln_done_x):
        for st in range(4):
            for half in range(2):
                bk = pool_next()
                w = [(c_v, ln_done_x[st]), bank_free[bk]]
                for eci in range(4):
                    ec = half * 4 + eci
                    pm.do("t", (lambda bk=bk, st=st, ec=ec, eci=eci: lambda e:
                                e.matmul(B[bk][:, eci * 128:(eci + 1) * 128],
                                         xn_tm[:, st * 1024 + ec * 128:
                                               st * 1024 + ec * 128 + 128],
                                         ident_b[:], start=(eci == 0),
                                         stop=(eci == 3)))(),
                          waits=(w if eci == 0 else ()),
                          incs=([(c_t, 1)] if eci == 3 else ()))
                bank_gen[bk] = c_t.n
                pm.do("v", (lambda bk=bk, st=st, half=half: lambda e:
                            e.tensor_copy(
                                xn_fm[:, 0:4096].rearrange(
                                    "p (ec f) -> p ec f", ec=8)[
                                    :, half * 4:half * 4 + 4,
                                    st * 128:st * 128 + 128],
                                B[bk][:].rearrange("p (a b) -> p a b", a=4)))(),
                      waits=[(c_t, bank_gen[bk]), xnfm_guard], incs=[(c_v, 1)])
                bank_free[bk] = (c_v, c_v.n)
        return c_v.n

    ln_sums_g[0] = (c_v, emit_ln_sums(0))
    ln_next = [emit_ln_tile(0, st, (c_t, 0)) for st in range(4)]
    xn_ready_next = emit_xnt(ln_next)

    for sc in range(nsc):
        par = sc % 2
        xb = x_buf[:, par * 4096:(par + 1) * 4096]

        if sc + 1 < nsc:
            npar = (sc + 1) % 2
            pm.do("y", (lambda sc=sc, npar=npar: lambda e: e.dma_start(
                        out=x_buf[:, npar * 4096:(npar + 1) * 4096].rearrange(
                            "p (st f) -> p st f", st=4),
                        in_=xr[sc + 1]))(),
                  waits=[x_guard[npar]], incs=[(d_xp[npar], 16)])

        ln_done = ln_next
        xn_ready = xn_ready_next

        for (wT, dst, brow) in ((wqT, q_fm, bq_eff), (wkT, k_fm, bk_f)):
            for fc in range(8):
                bk = fc % 4
                w = [(c_v, xn_ready), bank_free[bk], (c_s, setup_s)]
                for ec in range(8):
                    pm.do("t", (lambda bk=bk, wT=wT, fc=fc, ec=ec: lambda e:
                                e.matmul(B[bk][:],
                                         wT[:, ec * 1024 + fc * 128:
                                            ec * 1024 + fc * 128 + 128],
                                         xn_fm[:, ec * 512:(ec + 1) * 512],
                                         start=(ec == 0), stop=(ec == 7)))(),
                          waits=(w if ec == 0 else ()),
                          incs=([(c_t, 1)] if ec == 7 else ()))
                bank_gen[bk] = c_t.n
                pm.do("s", (lambda bk=bk, dst=dst, fc=fc, brow=brow: lambda e:
                            e.activation(dst[:, fc * 512:(fc + 1) * 512], B[bk][:],
                                         AF.Identity, bias=brow[:, fc:fc + 1]))(),
                      waits=[(c_t, bank_gen[bk]), kq_guard], incs=[(c_s, 1)])
                bank_free[bk] = (c_s, c_s.n)
        for st in range(4):
            for half in range(2):
                bk = (st * 2 + half) % 4
                w = [(c_v, xn_ready), bank_free[bk]]
                for ec in range(8):
                    pm.do("t", (lambda bk=bk, st=st, half=half, ec=ec: lambda e:
                                e.matmul(B[bk][:],
                                         xn_fm[:, ec * 512 + st * 128:
                                               ec * 512 + st * 128 + 128],
                                         wvT[:, ec * 1024 + half * 512:
                                             ec * 1024 + half * 512 + 512],
                                         start=(ec == 0), stop=False))(),
                          waits=(w if ec == 0 else ()))
                pm.do("t", (lambda bk=bk, half=half: lambda e:
                            e.matmul(B[bk][:], ones2[0:1, :],
                                     bv_row[0:1, half * 512:half * 512 + 512],
                                     start=False, stop=True))(),
                      incs=[(c_t, 1)])
                bank_gen[bk] = c_t.n
                pm.do("v", (lambda bk=bk, st=st, half=half: lambda e:
                            e.tensor_copy(v_tm[:, st * 1024 + half * 512:
                                               st * 1024 + half * 512 + 512],
                                          B[bk][:]))(),
                      waits=[(c_t, bank_gen[bk]), kq_guard], incs=[(c_v, 1)])
                bank_free[bk] = (c_v, c_v.n)
        v_done = c_v.n
        q_done_s = c_s.n

        if sc + 1 < nsc:
            ln_sums_g[0] = (c_v, emit_ln_sums(sc + 1))

        km_done = []
        for cc in range(4):
            c = 4 * sc + cc
            for fc in range(8):
                pm.do("v", (lambda fc=fc, cc=cc: lambda e:
                            e.reduce_sum(kcol[:], k_fm[:, fc * 512 + cc * 128:
                                                       fc * 512 + cc * 128 + 128],
                                         axis=AX.X))(),
                      waits=[(c_s, q_done_s), (c_v, c_v.n)], incs=[(c_v, 1)])
                pm.do("v", (lambda fc=fc, c=c: lambda e:
                            e.tensor_scalar(km_fm[:, fc * 64 + c:fc * 64 + c + 1],
                                            kcol[:], 1.0 / 128, None, ALU.mult))(),
                      waits=[c_v], incs=[(c_v, 1)])
            km_done.append(c_v.n)

        vm_latest = [[(d_vm[0], 0), (d_vm[1], 0)]]

        def emit_vsum(cc):
            c = 4 * sc + cc
            vr = vrow_slots[cc % 2]
            w = [(c_v, v_done), bank_free[6], bank_free[7], vrow_guard[cc % 2],
                 hflat_guard, a_guard]
            pm.do("t", (lambda cc=cc: lambda e:
                        e.matmul(B[6][0:1, :], ones_col[:],
                                 v_tm[:, cc * 1024:cc * 1024 + 512],
                                 start=True, stop=True))(),
                  waits=w, incs=[(c_t, 1)])
            pm.do("t", (lambda cc=cc: lambda e:
                        e.matmul(B[7][0:1, :], ones_col[:],
                                 v_tm[:, cc * 1024 + 512:cc * 1024 + 1024],
                                 start=True, stop=True))(),
                  incs=[(c_t, 1)])
            bank_gen[6] = bank_gen[7] = c_t.n
            pm.do("s", (lambda vr=vr: lambda e:
                        e.activation(vr[0:1, 0:512], B[6][0:1, :], AF.Copy))(),
                  waits=[(c_t, c_t.n)], incs=[(c_s, 1)])
            pm.do("s", (lambda vr=vr: lambda e:
                        e.activation(vr[0:1, 512:1024], B[7][0:1, :], AF.Copy))(),
                  incs=[(c_s, 1)])
            bank_free[6] = bank_free[7] = (c_s, c_s.n)
            sw = [c_s.snap()]
            pm.do("g", (lambda c=c, vr=vr: lambda e:
                        e.dma_start(out=vm_cm[c:c + 1, :], in_=vr))(),
                  waits=sw, incs=[(d_vm[cc % 2], 16)])
            pm.do("g", (lambda c=c, vr=vr: lambda e:
                        e.dma_start(out=vm_cm[64 + c:65 + c, :], in_=vr))(),
                  incs=[(d_vm[cc % 2], 16)])
            vrow_guard[cc % 2] = (d_vm[cc % 2], d_vm[cc % 2].n)
            vm_latest[0] = [(d_vm[0], d_vm[0].n), (d_vm[1], d_vm[1].n)]
            return vm_latest[0]


        def emit_p1(cc):
            c = 4 * sc + cc
            ncj = c + 1
            first = [(c_s, q_done_s), (c_v, km_done[cc]), a_guard]
            for kind, g in (("i", 0), ("i", 1), ("n", 0), ("i", 2), ("i", 3),
                            ("n", 1)):
                bk = pool_next()
                w = [bank_free[bk]] + first
                first = []
                if kind == "i":
                    for qi, h in enumerate(grp_heads[g]):
                        base = (h % 2) * 64
                        fc = h // 2
                        pm.do("t", (lambda bk=bk, qi=qi, base=base, fc=fc,
                                    cc=cc: lambda e: e.matmul(
                                        B[bk][:, qi * 128:(qi + 1) * 128],
                                        q_fm[base:base + 64, fc * 512 + cc * 128:
                                             fc * 512 + cc * 128 + 128],
                                        k_fm[base:base + 64, fc * 512 + cc * 128:
                                             fc * 512 + cc * 128 + 128],
                                        start=(qi == 0), stop=(qi == 3),
                                        skip_group_check=True))(),
                              waits=(w if qi == 0 else ()),
                              incs=([(c_t, 1)] if qi == 3 else ()))
                    pm.do("s", (lambda bk=bk: lambda e:
                                e.activation(B[bk][:], B[bk][:], AF.Exp,
                                             bias=cvals[:, 5:6]))(),
                          waits=[(c_t, c_t.n)], incs=[(c_s, 1)])
                    pm.do("v", (lambda bk=bk, g=g: lambda e:
                                e.reduce_sum(z1[:, 4 * g:4 * g + 4],
                                             B[bk][:].rearrange(
                                                 "p (a b) -> p a b", a=4),
                                             axis=AX.X))(),
                          waits=[(c_s, c_s.n)], incs=[(c_v, 1)])
                else:
                    par8 = g
                    hs = [h for h in range(16) if h % 2 == par8]
                    for j, h in enumerate(hs):
                        base = (h % 2) * 64
                        fc = h // 2
                        sl = (cix(h) % 8) * ncj
                        pm.do("t", (lambda bk=bk, sl=sl, base=base, fc=fc,
                                    cc=cc, ncj=ncj, j=j: lambda e: e.matmul(
                                        B[bk][:, sl:sl + ncj],
                                        q_fm[base:base + 64, fc * 512 + cc * 128:
                                             fc * 512 + cc * 128 + 128],
                                        km_fm[base:base + 64,
                                              fc * 64:fc * 64 + ncj],
                                        start=(j == 0), stop=(j == 7),
                                        skip_group_check=True))(),
                              waits=(w if j == 0 else ()),
                              incs=([(c_t, 1)] if j == 7 else ()))
                    if debug:
                        pm.do("v", (lambda bk=bk, g=g, ncj=ncj: lambda e:
                                    e.reduce_sum(negm2[:, 8 * g:8 * g + 8],
                                                 B[bk][:, 0:8 * ncj].rearrange(
                                                     "p (a b) -> p a b", a=8),
                                                 axis=AX.X))(),
                              waits=[(c_t, c_t.n), (c_v, c_v.n)],
                              incs=[(c_v, 1)])
                    pm.do("s", (lambda bk=bk, ncj=ncj: lambda e:
                                e.activation(B[bk][:, 0:8 * ncj],
                                             B[bk][:, 0:8 * ncj], AF.Exp,
                                             bias=cvals[:, 5:6]))(),
                          waits=[(c_t, c_t.n), (c_v, c_v.n)], incs=[(c_s, 1)])
                    pm.do("v", (lambda bk=bk, g=g, ncj=ncj: lambda e:
                                e.reduce_sum(z2[:, 8 * g:8 * g + 8],
                                             B[bk][:, 0:8 * ncj].rearrange(
                                                 "p (a b) -> p a b", a=8),
                                             axis=AX.X))(),
                          waits=[(c_v, c_v.n), (c_s, c_s.n)], incs=[(c_v, 1)])
                    if debug:
                        for q8 in range(8):
                            pm.do("v", (lambda bk=bk, q8=q8, g=g, ncj=ncj:
                                        lambda e: e.reduce_sum(
                                            z2b[:, 8 * g + q8:8 * g + q8 + 1],
                                            B[bk][:, q8 * ncj:(q8 + 1) * ncj],
                                            axis=AX.X))(),
                                  waits=[c_v], incs=[(c_v, 1)])
                bank_free[bk] = (c_v, c_v.n)
            zred_v = c_v.n
            pm.do("s", lambda e: e.activation(lnz[:, 0:16], z1[:], AF.Ln),
                  waits=[(c_v, zred_v)], incs=[(c_s, 1)])
            pm.do("s", lambda e: e.activation(lnz[:, 16:32], z2[:], AF.Ln,
                                              scale=cvals[:, 4:5]),
                  incs=[(c_s, 1)])
            sw = [c_s.snap()]
            pm.do("v", lambda e: e.tensor_scalar(neglse[:], lnz[:], -1.0, -20.0,
                                                 ALU.mult, ALU.add),
                  waits=sw + [c_v.snap(), neglse_guard[0]], incs=[(c_v, 1)])
            return c_v.n

        def emit_lse_bounce(cc, neglse_v):
            bk = pool_next()
            pm.do("t", (lambda bk=bk: lambda e:
                        e.transpose(B[bk][0:32, 0:128], neglse[:], ident[:]))(),
                  waits=[(c_v, neglse_v), bank_free[bk]], incs=[(c_t, 1)])
            bank_gen[bk] = c_t.n
            neglse_guard[0] = (c_t, c_t.n)
            pm.do("v", (lambda bk=bk: lambda e:
                        e.tensor_copy(rows_cd[:, 0:128], B[bk][0:32, 0:128]))(),
                  waits=[(c_t, c_t.n), d_lst], incs=[(c_v, 1)])
            pm.do("v", (lambda bk=bk: lambda e:
                        e.tensor_tensor(rows_cd[:, 128:256], B[bk][0:32, 0:128],
                                        rows_cd[:, 0:128], ALU.subtract))(),
                  waits=[c_v], incs=[(c_v, 1)])
            rows_v = c_v.n
            bank_free[bk] = (c_v, rows_v)
            scr = lse_scr[cc % 2]
            pm.do("g", (lambda scr=scr: lambda e:
                        e.dma_start(out=scr, in_=rows_cd[:]))(),
                  waits=[(c_v, rows_v), lse_guard[cc % 2]], incs=[(d_lst, 16)])
            fi, fb = flat_bufs[cc % 2]
            pm.do("g", (lambda scr=scr, fi=fi: lambda e: e.dma_start(
                        out=fi.rearrange("two (h t) -> two h t", h=16),
                        in_=scr[0:16, :].rearrange("h (two t) -> two h t",
                                                   two=2)))(),
                  waits=[d_lst, flat_guard[cc % 2], hflat_guard, pq_guard],
                  incs=[(d_lsl, 16)])
            scr5 = scr[16:32, :].rearrange(
                "(half q2 q4) (two t) -> q2 two half q4 t", half=2, q2=2, two=2)
            for q2i in range(2):
                for twoi in range(2):
                    r = 2 * twoi + q2i
                    pm.do("g", (lambda fb=fb, r=r, scr5=scr5, q2i=q2i,
                                twoi=twoi: lambda e: e.dma_start(
                                    out=fb[r:r + 1, :].rearrange(
                                        "o (half q4 t) -> o half q4 t",
                                        half=2, q4=4),
                                    in_=scr5[q2i, twoi]))(),
                          incs=[(d_lsl, 16)])
            return d_lsl.n

        def emit_p2_av(cc, flat_v, part):
            c = 4 * sc + cc
            ncj = c + 1
            fi, fb = flat_bufs[cc % 2]
            o2 = ones2_sl[cc % 2]
            s4 = sel4_sl[cc % 2]
            exp_done = exp_done_st[0]
            avd = (c_t, av_done_prev[0])
            gens = ((("i", 0), ("i", 1), ("n", 0)) if part == 0 else
                    (("i", 2), ("i", 3), ("n", 1)))
            if part == 2:
                gens = ()
            for kind, g in gens:
                bk = pool_next()
                if kind == "i":
                    for qi, h in enumerate(grp_heads[g]):
                        base = (h % 2) * 64
                        fc = h // 2
                        pm.do("t", (lambda bk=bk, qi=qi, base=base, fc=fc,
                                    cc=cc: lambda e: e.matmul(
                                        B[bk][:, qi * 128:(qi + 1) * 128],
                                        k_fm[base:base + 64, fc * 512 + cc * 128:
                                             fc * 512 + cc * 128 + 128],
                                        q_fm[base:base + 64, fc * 512 + cc * 128:
                                             fc * 512 + cc * 128 + 128],
                                        start=(qi == 0), stop=False,
                                        skip_group_check=True))(),
                              waits=([bank_free[bk]] if qi == 0 else ()))
                    pm.do("t", (lambda bk=bk, g=g, fi=fi, o2=o2: lambda e:
                                e.matmul(B[bk][:], o2,
                                         fi[:, g * 512:(g + 1) * 512],
                                         start=False, stop=True,
                                         skip_group_check=True))(),
                          waits=[(d_lsl, flat_v)], incs=[(c_t, 1)])
                else:
                    par8 = g
                    pm.do("t", (lambda bk=bk, g=g, fb=fb, s4=s4: lambda e:
                                e.matmul(B[bk][:], s4,
                                         fb[:, g * 512:(g + 1) * 512],
                                         start=True, stop=False,
                                         skip_group_check=True))(),
                          waits=[bank_free[bk], (d_lsl, flat_v)])
                    hs = [h for h in range(16) if h % 2 == par8]
                    for j, h in enumerate(hs):
                        base = (h % 2) * 64
                        fc = h // 2
                        q8 = cix(h) % 8
                        sl = (q8 % 4) * 128
                        pb = 64 * (q8 // 4)
                        pm.do("t", (lambda bk=bk, sl=sl, pb=pb, base=base,
                                    fc=fc, cc=cc, ncj=ncj, j=j: lambda e: e.matmul(
                                        B[bk][pb:pb + ncj, sl:sl + 128],
                                        km_fm[base:base + 64,
                                              fc * 64:fc * 64 + ncj],
                                        q_fm[base:base + 64, fc * 512 + cc * 128:
                                             fc * 512 + cc * 128 + 128],
                                        start=False, stop=(j == 7),
                                        skip_group_check=True))(),
                              incs=([(c_t, 1)] if j == 7 else ()))
                if kind == "i":
                    pm.do("s", (lambda bk=bk, g=g: lambda e:
                                e.activation(A1T[:, g * 512:(g + 1) * 512],
                                             B[bk][:], AF.Exp))(),
                          waits=[(c_t, c_t.n), avd] + vm_latest[0],
                          incs=[(c_s, 1)])
                else:
                    pm.do("s", (lambda bk=bk, g=g: lambda e:
                                e.activation(A2T[:, g * 512:(g + 1) * 512],
                                             B[bk][:], AF.Exp))(),
                          waits=[(c_t, c_t.n), avd] + vm_latest[0],
                          incs=[(c_s, 1)])
                exp_done[(kind, g)] = c_s.n
                bank_free[bk] = (c_s, c_s.n)

            if part != 2:
                return
            first_in_bank = {}
            av_first = True
            for h in [hh for hh in range(16) if hh % 2 == 0] + \
                     [hh for hh in range(16) if hh % 2 == 1]:
                ci = cix(h)
                ec = h // 2
                abk = 6 + ec // 4
                qq = ec % 4
                pbase = (h % 2) * 64
                w = []
                if av_first:
                    w = [bank_free[6], bank_free[7]] + vm_ready
                    av_first = False
                if ci % 4 == 0:
                    w = w + [(c_s, exp_done[("i", ci // 4)])]
                stf = not first_in_bank.get((abk, pbase), False)
                pm.do("t", (lambda abk=abk, qq=qq, pbase=pbase, h=h, cc=cc,
                            ci=ci, stf=stf: lambda e: e.matmul(
                                B[abk][pbase:pbase + 64, qq * 128:(qq + 1) * 128],
                                v_tm[:, cc * 1024 + h * 64:cc * 1024 + h * 64 + 64],
                                A1T[:, ci * 128:(ci + 1) * 128],
                                start=stf, stop=False,
                                skip_group_check=True))(),
                      waits=w)
                first_in_bank[(abk, pbase)] = True
            for h in [hh for hh in range(16) if hh % 2 == 0] + \
                     [hh for hh in range(16) if hh % 2 == 1]:
                ci = cix(h)
                ec = h // 2
                abk = 6 + ec // 4
                qq = ec % 4
                pbase = (h % 2) * 64
                q8 = ci % 8
                pb2 = 64 * (q8 // 4)
                sl = (ci // 8) * 512 + (q8 % 4) * 128
                wi = []
                if h == 0:
                    wi = [(c_s, exp_done[("n", 0)])]
                elif h == 1:
                    wi = [(c_s, exp_done[("n", 1)])]
                pm.do("t", (lambda abk=abk, qq=qq, pbase=pbase, h=h, ncj=ncj,
                            sl=sl, pb2=pb2, cc=cc: lambda e: e.matmul(
                                B[abk][pbase:pbase + 64, qq * 128:(qq + 1) * 128],
                                vm_cm[pb2:pb2 + ncj, h * 64:h * 64 + 64],
                                A2T[pb2:pb2 + ncj, sl:sl + 128],
                                start=False, stop=False,
                                skip_group_check=True))(),
                      waits=wi,
                      incs=([(c_t, 1)] if h == 15 else ()))
            av_done_prev[0] = c_t.n
            flat_guard[cc % 2] = (c_t, c_t.n)
            lse_guard[cc % 2] = (c_t, c_t.n)
            for abk in (6, 7):
                pm.do("v", (lambda abk=abk, cc=cc: lambda e: e.tensor_copy(
                            attn_fm[:, 0:4096].rearrange("p (ec f) -> p ec f",
                                                         ec=8)[
                                :, (abk - 6) * 4:(abk - 6) * 4 + 4,
                                cc * 128:cc * 128 + 128],
                            B[abk][:].rearrange("p (a b) -> p a b", a=4)))(),
                      waits=[(c_t, av_done_prev[0]), attn_guard],
                      incs=[(c_v, 1)])
                bank_free[abk] = (c_v, c_v.n)

        for cc in range(4):
            emit_vsum(cc)
        vm_ready_cc = {cc: vm_latest[0] for cc in range(4)}
        nl0 = emit_p1(0)
        fl0 = emit_lse_bounce(0, nl0)
        flat_v = {0: fl0}
        for rc in range(2):
            bk = pool_next()
            w = [(c_v, xn_ready), bank_free[bk]]
            for dc in range(8):
                pm.do("t", (lambda bk=bk, rc=rc, dc=dc: lambda e: e.matmul(
                            B[bk][:],
                            U1[:, dc * 256 + rc * 128:dc * 256 + rc * 128 + 128],
                            xn_fm[:, dc * 512:(dc + 1) * 512],
                            start=(dc == 0), stop=(dc == 7)))(),
                      waits=(w if dc == 0 else ()),
                      incs=([(c_t, 1)] if dc == 7 else ()))
            bank_gen[bk] = c_t.n
            pm.do("v", (lambda bk=bk, rc=rc: lambda e:
                        e.tensor_copy(P_fm[:, rc * 512:(rc + 1) * 512], B[bk][:]))(),
                  waits=[(c_t, bank_gen[bk]), pq_guard], incs=[(c_v, 1)])
            bank_free[bk] = (c_v, c_v.n)
        p_done_sc = c_v.n
        xnfm_guard = (c_t, c_t.n)
        for cc in range(4):
            if cc + 1 < 4:
                nl = emit_p1(cc + 1)
            vm_ready = vm_ready_cc[cc]
            exp_done_st[0] = {}
            emit_p2_av(cc, flat_v[cc], 0)
            if cc + 1 < 4:
                flat_v[cc + 1] = emit_lse_bounce(cc + 1, nl)
            emit_p2_av(cc, flat_v[cc], 1)
            emit_p2_av(cc, flat_v[cc], 2)
            if sc + 1 < nsc:
                ln_next[cc] = emit_ln_tile(sc + 1, cc,
                                           (c_t, av_done_prev[0]))
        a_guard = (c_t, av_done_prev[0])
        attn_done_v = c_v.n
        kq_guard = (c_t, c_t.n)

        p_done = p_done_sc

        h_gelu = [0] * 32
        qw_first = True
        for g in range(34):
            if g < 32:
                bk = 2 + g % 4
                w = [(c_v, p_done), bank_free[bk]]
                for rc in range(2):
                    pm.do("t", (lambda bk=bk, g=g, rc=rc: lambda e: e.matmul(
                                B[bk][:],
                                V1[:, rc * 4096 + g * 128:rc * 4096 + g * 128 + 128],
                                P_fm[:, rc * 512:(rc + 1) * 512],
                                start=(rc == 0), stop=(rc == 1)))(),
                          waits=(w if rc == 0 else ()),
                          incs=([(c_t, 1)] if rc == 1 else ()))
                bank_gen[bk] = c_t.n
                pm.do("s", (lambda bk=bk, g=g: lambda e: e.activation(
                            H_rot[:, (g % 4) * 512:(g % 4) * 512 + 512],
                            B[bk][:], AF.Gelu_apprx_tanh,
                            bias=b1_f[:, g:g + 1]))(),
                      waits=[(c_t, bank_gen[bk]), hrot_guard[g % 4],
                             vrow_guard[0], vrow_guard[1],
                             flat_guard[0], flat_guard[1]],
                      incs=[(c_s, 1)])
                h_gelu[g] = c_s.n
                bank_free[bk] = (c_s, c_s.n)
            gq = g - 2
            if 0 <= gq < 32:
                w = [(c_s, h_gelu[gq])]
                if qw_first:
                    w += [bank_free[6], bank_free[7], (c_v, attn_done_v)]
                for rc in range(2):
                    pm.do("t", (lambda gq=gq, rc=rc: lambda e: e.matmul(
                                B[6 + rc][:],
                                U2[:, gq * 256 + rc * 128:gq * 256 + rc * 128 + 128],
                                H_rot[:, (gq % 4) * 512:(gq % 4) * 512 + 512],
                                start=(gq == 0), stop=(gq == 31),
                                skip_group_check=True))(),
                          waits=(w if rc == 0 else ()),
                          incs=([(c_t, 1)] if (gq == 31 and rc == 1) else ()))
                if qw_first:
                    qw_first = False
                hrot_guard[gq % 4] = (c_t, c_t.n)
        q_mm_done = c_t.n
        hflat_guard = (c_t, q_mm_done)
        if sc + 1 < nsc:
            xn_ready_next = emit_xnt(ln_next)
        bank_gen[6] = bank_gen[7] = c_t.n
        for rc in range(2):
            pm.do("v", (lambda rc=rc: lambda e:
                        e.tensor_copy(Q_fm[:, rc * 512:(rc + 1) * 512],
                                      B[6 + rc][:]))(),
                  waits=[(c_t, q_mm_done)], incs=[(c_v, 1)])
            bank_free[6 + rc] = (c_v, c_v.n)
        pq_guard = (c_t, 0)

        for st in range(4):
            bks = [st % 2 * 2, st % 2 * 2 + 1]
            slot = st % 2
            for half in range(2):
                bk = bks[half]
                w = [(c_v, attn_done_v), (c_v, c_v.n), bank_free[bk]]
                for ec in range(8):
                    pm.do("t", (lambda bk=bk, st=st, half=half, ec=ec: lambda e:
                                e.matmul(
                                    B[bk][:],
                                    attn_fm[:, ec * 512 + st * 128:
                                            ec * 512 + st * 128 + 128],
                                    woT[:, ec * 1024 + half * 512:
                                        ec * 1024 + half * 512 + 512],
                                    start=(ec == 0), stop=False))(),
                          waits=(w if ec == 0 else ()))
                for rc in range(2):
                    pm.do("t", (lambda bk=bk, st=st, half=half, rc=rc: lambda e:
                                e.matmul(
                                    B[bk][:],
                                    Q_fm[:, rc * 512 + st * 128:
                                         rc * 512 + st * 128 + 128],
                                    V2[:, rc * 1024 + half * 512:
                                       rc * 1024 + half * 512 + 512],
                                    start=False, stop=False))())
                pm.do("t", (lambda bk=bk, half=half: lambda e: e.matmul(
                            B[bk][:], ones2[0:1, :],
                            bob2_row[0:1, half * 512:half * 512 + 512],
                            start=False, stop=True))(),
                      incs=[(c_t, 1)])
                bank_gen[bk] = c_t.n
                pm.do("v", (lambda bk=bk, st=st, half=half, slot=slot, xb=xb:
                            lambda e: e.tensor_tensor(
                                out_st[:, slot * 1024 + half * 512:
                                       slot * 1024 + half * 512 + 512],
                                B[bk][:],
                                xb[:, st * 1024 + half * 512:
                                   st * 1024 + half * 512 + 512],
                                ALU.add))(),
                      waits=[(c_t, bank_gen[bk]), stage_guard[slot]],
                      incs=[(c_v, 1)])
                bank_free[bk] = (c_v, c_v.n)
            ev = c_v.n
            pm.do("y", (lambda sc=sc, st=st, slot=slot: lambda e: e.dma_start(
                        out=out_h[sc * 512 + st * 128:sc * 512 + st * 128 + 128, :],
                        in_=out_st[:, slot * 1024:slot * 1024 + 1024]))(),
                  waits=[(c_v, ev)], incs=[(d_op[slot], 16)])
            stage_guard[slot] = (d_op[slot], d_op[slot].n)
        x_guard[par] = (c_v, c_v.n)
        pq_guard = (c_t, c_t.n)
        attn_guard = (c_v, c_v.n)


    if debug:
        d_dbg = Ctr(nc, "d_dbg")
        dbg_list = [("dbg_xnfm", xn_fm), ("dbg_q", q_fm), ("dbg_k", k_fm),
                    ("dbg_v", v_tm), ("dbg_km", km_fm), ("dbg_vm", vm_cm),
                    ("dbg_attn", attn_fm), ("dbg_P", P_fm), ("dbg_Q", Q_fm),
                    ("dbg_negm1", negm1), ("dbg_negm2", negm2),
                    ("dbg_z1", z1), ("dbg_z2", z2), ("dbg_z2b", z2b), ("dbg_neglse", neglse),
                    ("dbg_A1T", A1T), ("dbg_A2T", A2T), ("dbg_rowscd", rows_cd),
                    ("dbg_bqeff", bq_eff), ("dbg_bkf", bk_f), ("dbg_b1f", b1_f),
                    ("dbg_bob2", bob2_row), ("dbg_bvrow", bv_row),
                    ("dbg_wqT", wqT), ("dbg_U1", U1)]
        allw = [c_t, c_v, c_s]
        for (nm, tt) in dbg_list:
            od = ap(nc.dram_tensor(nm, list(tt.shape), tt.dtype,
                                   kind="ExternalOutput"))
            pm.do("g", (lambda od=od, tt=tt: lambda e:
                        e.dma_start(out=od, in_=tt[:]))(),
                  waits=[c_t, c_v, c_s], incs=[(d_dbg, 16)])

    fin_w = [d_op[0], d_op[1], d_lst, d_lsl, d_vm[0], d_vm[1]]
    if debug:
        pm.do("g", lambda e: e.nop(), waits=[d_dbg])
    pm.do("y", lambda e: e.nop(), waits=fin_w)

    with nc.Block("main") as blk2:
        pm.emit(blk2)

    return nc


_NC_CACHE = {}


def kernel(**inputs):
    nsc = NSC_FULL
    if nsc not in _NC_CACHE:
        _NC_CACHE[nsc] = build_nc(nsc)
    nc = _NC_CACHE[nsc]
    x = np.ascontiguousarray(inputs["L0_attn_hidden_states"])
    weights = {k: np.ascontiguousarray(v) for k, v in inputs.items()
               if k != "L0_attn_hidden_states"}
    in_maps = []
    for core in range(NCORE):
        m = dict(weights)
        m["L0_attn_hidden_states"] = x[core]
        in_maps.append(m)
    res = run_bass_kernel_spmd(nc, in_maps, core_ids=list(range(NCORE)))
    return np.stack([res.results[i]["out"] for i in range(NCORE)], axis=0)
